# version 12
# baseline (speedup 1.0000x reference)
import numpy as np
import ml_dtypes
import concourse.bass as bass
import concourse.mybir as mybir
from concourse.bass_utils import run_bass_kernel_spmd

F32 = mybir.dt.float32
BF16 = mybir.dt.bfloat16
I32 = mybir.dt.int32
AF = mybir.ActivationFunctionType
OP = mybir.AluOpType
AX = mybir.AxisListType

FUSED = True

NCORES = 8
T = 2048
NSEQ = 2
C = 1024
CC = 8
HID = 3072
NHT = HID // 128
NEG = -1.0e30
DQS = 1.0 / 127.0

COMPUTE = ("tensor", "vector", "scalar", "gpsimd")
SEMOF = {"tensor": "t", "vector": "v", "scalar": "s", "gpsimd": "g"}
DMASEMS = ("ds", "dg", "da0", "da1", "da2", "da3", "dv0", "dv1", "dg1")


class Prog:

    def __init__(self):
        self.ops = {e: [] for e in COMPUTE + ("sync",)}
        self.cnt = {"t": 0, "v": 0, "s": 0, "g": 0}
        for sm in DMASEMS:
            self.cnt[sm] = 0
        self.mark = {}
        self.sem = {}
        self.wtok = {}
        self.rtok = {}
        self.inherit = {}

    def _wait(self, eng, tok):
        if tok is None:
            return
        sm, v = tok
        if sm == "t" and eng == "tensor":
            return
        key = (eng, sm)
        if self.mark.get(key, -1) >= v:
            return
        self.mark[key] = v
        self.ops[eng].append(lambda e, sm=sm, v=v: e.wait_ge(self.sem[sm], v))

    def _pre(self, eng, reads, writes):
        for r in reads:
            self._wait(eng, self.wtok.get(r))
        for w in writes:
            self._wait(eng, self.wtok.get(w))
            for t in self.rtok.get(w, ()):
                self._wait(eng, t)
            for t in self.inherit.pop(w, ()):
                self._wait(eng, t)

    def _post(self, tok, reads, writes):
        for w in writes:
            self.wtok[w] = tok
            self.rtok[w] = []
        for r in reads:
            self.rtok.setdefault(r, []).append(tok)

    def step(self, eng, fns, reads=(), writes=()):
        if not isinstance(fns, (list, tuple)):
            fns = [fns]
        self._pre(eng, reads, writes)
        sm = SEMOF[eng]
        self.cnt[sm] += 1
        v = self.cnt[sm]
        if eng in ("gpsimd", "scalar", "vector", "tensor"):
            for fn in fns:
                self.ops[eng].append(fn)
            self.ops[eng].append(lambda e, sm=sm: e.drain().then_inc(self.sem[sm], 1))
        else:
            for fn in fns[:-1]:
                self.ops[eng].append(fn)
            last = fns[-1]
            self.ops[eng].append(
                lambda e, last=last, sm=sm: last(e).then_inc(self.sem[sm], 1))
        tok = (sm, v)
        self._post(tok, reads, writes)
        return tok

    def dstep(self, queue, transfers, reads=(), writes=(), sem=None):
        sm = sem or ("ds" if queue == "sync" else "dg")
        self._wait(queue, (sm, self.cnt[sm]))
        self._pre(queue, reads, writes)
        for out, in_ in transfers:
            self.cnt[sm] += 16
            self.ops[queue].append(
                lambda e, out=out, in_=in_, sm=sm:
                    e.dma_start(out=out, in_=in_).then_inc(self.sem[sm], 16))
        tok = (sm, self.cnt[sm])
        self._post(tok, reads, writes)
        return tok

    def fence(self):
        toks = []
        for eng in COMPUTE:
            sm = SEMOF[eng]
            self.cnt[sm] += 1
            self.ops[eng].append(lambda e, sm=sm: e.nop().then_inc(self.sem[sm], 1))
            toks.append((sm, self.cnt[sm]))
        for sm in DMASEMS:
            toks.append((sm, self.cnt[sm]))
        for eng in COMPUTE + ("sync",):
            for t in toks:
                if t[0] != SEMOF.get(eng):
                    key = (eng, t[0])
                    if self.mark.get(key, -1) < t[1]:
                        self.mark[key] = t[1]
                        self.ops[eng].append(
                            lambda e, t=t: e.wait_ge(self.sem[t[0]], t[1]))


class Arena:

    def __init__(self, p, ap, nbytes):
        self.p = p
        self.ap = ap
        self.nbytes = nbytes
        self.blocks = {}
        self.ghosts = []
        self.peak = 0

    def alloc(self, name, parts, free, dtype):
        assert name not in self.blocks, name
        sz = (free * mybir.dt.size(dtype) + 31) & ~31
        off = 0
        for boff, bsz in sorted(self.blocks.values()):
            if off + sz <= boff:
                break
            off = max(off, boff + bsz)
        if off + sz > self.nbytes:
            raise MemoryError(f"arena OOM: {name} {sz}B at {off}")
        self.blocks[name] = (off, sz)
        self.peak = max(self.peak, off + sz)
        toks = []
        for goff, gsz, gtoks in self.ghosts:
            if goff < off + sz and off < goff + gsz:
                toks.extend(gtoks)
        if toks:
            self.p.inherit.setdefault(name, []).extend(toks)
        return self.ap[0:parts, off:off + free * mybir.dt.size(dtype)].bitcast(dtype)

    def free(self, name):
        off, sz = self.blocks.pop(name)
        toks = list(self.p.rtok.get(name, ()))
        if self.p.wtok.get(name) is not None:
            toks.append(self.p.wtok[name])
        toks.extend(self.p.inherit.pop(name, ()))
        self.p.wtok.pop(name, None)
        self.p.rtok.pop(name, None)
        if toks:
            self.ghosts.append((off, sz, toks))

    def alloc_many(self, base, n, parts, free, dtype):
        return [self.alloc(f"{base}{i}", parts, free, dtype) for i in range(n)]

    def free_many(self, base, n):
        for i in range(n):
            self.free(f"{base}{i}")



def build_program(items):
    import contextlib
    nc = bass.Bass("TRN2", target_bir_lowering=False)
    phases = {ph for ph, _ in items}

    d = {}
    d["x_in"] = nc.dram_tensor("x_in", [C, NSEQ * T], F32, kind="ExternalInput").ap()
    d["y_out"] = nc.dram_tensor("y_out", [C, NSEQ * T], F32, kind="ExternalOutput").ap()
    d["ones_row"] = nc.dram_tensor("ones_row", [1, 128], F32, kind="ExternalInput").ap()
    d["negones_row"] = nc.dram_tensor("negones_row", [1, 128], F32, kind="ExternalInput").ap()
    d["onesbf_row"] = nc.dram_tensor("onesbf_row", [1, 128], BF16, kind="ExternalInput").ap()
    d["meancol"] = nc.dram_tensor("meancol", [128, 1], BF16, kind="ExternalInput").ap()
    d["headsel"] = nc.dram_tensor("headsel", [128, 4], BF16, kind="ExternalInput").ap()
    d["cmask"] = nc.dram_tensor("cmask", [512, 512], F32, kind="ExternalInput").ap()
    if "l0ffn" in phases:
        for n in ("w_r", "w_k", "w_v"):
            d[n] = nc.dram_tensor(n, [C, C], F32, kind="ExternalInput").ap()
            d[n + "_h"] = nc.dram_tensor(n + "_h", [CC, 128, 1024], BF16,
                                         kind="Internal").ap()
        d["mu_r"] = nc.dram_tensor("mu_r", [128, CC], F32, kind="ExternalInput").ap()
        d["mu_k"] = nc.dram_tensor("mu_k", [128, CC], F32, kind="ExternalInput").ap()
    if "l0attn" in phases:
        d["wq0"] = nc.dram_tensor("wq0", [512, 512], F32, kind="ExternalInput").ap()
        d["wk0"] = nc.dram_tensor("wk0", [512, 512], F32, kind="ExternalInput").ap()
        d["wv0"] = nc.dram_tensor("wv0", [512, 512], F32, kind="ExternalInput").ap()
        d["wo0"] = nc.dram_tensor("wo0", [512, C], F32, kind="ExternalInput").ap()
        d["bo0"] = nc.dram_tensor("bo0", [128, CC], F32, kind="ExternalInput").ap()
        d["wq0_h"] = nc.dram_tensor("wq0_h", [128, 2048], BF16, kind="Internal").ap()
        d["wk0_h"] = nc.dram_tensor("wk0_h", [128, 2048], BF16, kind="Internal").ap()
        d["wv0_h"] = nc.dram_tensor("wv0_h", [128, 2048], BF16, kind="Internal").ap()
        d["wo0_h"] = nc.dram_tensor("wo0_h", [4, 128, 1024], BF16, kind="Internal").ap()
    if "l1attn" in phases:
        d["wq1"] = nc.dram_tensor("wq1", [C, C], F32, kind="ExternalInput").ap()
        d["wk1"] = nc.dram_tensor("wk1", [C, C], F32, kind="ExternalInput").ap()
        d["wv1"] = nc.dram_tensor("wv1", [C, C], F32, kind="ExternalInput").ap()
        d["bq1"] = nc.dram_tensor("bq1", [128, CC], F32, kind="ExternalInput").ap()
        d["bk1"] = nc.dram_tensor("bk1", [128, CC], F32, kind="ExternalInput").ap()
        d["bv1"] = nc.dram_tensor("bv1", [1, C], F32, kind="ExternalInput").ap()
        d["wq1_h"] = nc.dram_tensor("wq1_h", [CC, 128, 1024], BF16, kind="Internal").ap()
        d["wk1_h"] = nc.dram_tensor("wk1_h", [CC, 128, 1024], BF16, kind="Internal").ap()
        d["wv1_h"] = nc.dram_tensor("wv1_h", [CC, 128, 1024], BF16, kind="Internal").ap()
    if "l1ffn" in phases:
        d["wg"] = nc.dram_tensor("wg", [C, HID], I32, kind="ExternalInput").ap()
        d["wu"] = nc.dram_tensor("wu", [C, HID], I32, kind="ExternalInput").ap()
        d["wd"] = nc.dram_tensor("wd", [HID, C], I32, kind="ExternalInput").ap()
        d["wgbf"] = nc.dram_tensor("wgbf", [NHT, 128, 1024], BF16, kind="Internal").ap()
        d["wubf"] = nc.dram_tensor("wubf", [NHT, 128, 1024], BF16, kind="Internal").ap()
        d["wdbf"] = nc.dram_tensor("wdbf", [NHT, 128, 1024], BF16, kind="Internal").ap()
    if "l1ffn" in phases or "l1attn" in phases:
        d["xspill"] = nc.dram_tensor("xspill", [C, T], F32, kind="Internal").ap()

    p = Prog()
    ARENA_B = 206 * 1024
    with contextlib.ExitStack() as stack:
        arena_t = stack.enter_context(
            nc.sbuf_tensor("arena", [128, ARENA_B], mybir.dt.uint8))
        pbt = [stack.enter_context(nc.psum_tensor(f"pb{i}", [128, 512], F32))
               for i in range(8)]
        for sm in ("t", "v", "s", "g") + DMASEMS:
            p.sem[sm] = stack.enter_context(nc.semaphore(f"sem_{sm}"))
        block = stack.enter_context(nc.Block())
        pb = [x.ap() if hasattr(x, "ap") else x for x in pbt]
        ar = Arena(p, arena_t.ap(), ARENA_B)
        plan_all(p, ar, pb, d, items)
        for eng_name, section in (("sync", block.sync), ("tensor", block.tensor),
                                  ("vector", block.vector), ("scalar", block.scalar),
                                  ("gpsimd", block.gpsimd)):
            ops = p.ops[eng_name]
            def body(e, ops=ops):
                for fn in ops:
                    fn(e)
            section(body)
    return nc


def PN(i):
    return f"pb{i}"


def plan_all(p, ar, pb, d, items):
    phases = {ph for ph, _ in items}
    g = {}
    specs = [("ones_row", 1, 128, F32), ("negones_row", 1, 128, F32),
             ("onesbf_row", 1, 128, BF16), ("meancol", 128, 1, BF16)]
    if "l0attn" in phases:
        specs.append(("headsel", 128, 4, BF16))
    for nm, parts, free, dt in specs:
        g[nm] = ar.alloc(nm, parts, free, dt)
    xfers = [(g["ones_row"], d["ones_row"][:, :]),
             (g["negones_row"], d["negones_row"][:, :]),
             (g["onesbf_row"], d["onesbf_row"][:, :]),
             (g["meancol"], d["meancol"][:, :])]
    if "l0attn" in phases:
        xfers.append((g["headsel"], d["headsel"][:, :]))
    p.dstep("sync", xfers, writes=[s[0] for s in specs])

    st = {"x": ar.alloc_many("x_", CC, 128, T, F32),
          "first": True, "fA0": True, "fF1": True, "fA1": True}
    cur = None
    for idx, (ph, s) in enumerate(items):
        if s != cur:
            for cc in range(CC):
                if cur is not None:
                    p.dstep("gpsimd",
                            [(d["y_out"][cc * 128:(cc + 1) * 128,
                                         cur * T:(cur + 1) * T], st["x"][cc])],
                            reads=[f"x_{cc}"], writes=["y_out"])
                p.dstep("sync",
                        [(st["x"][cc],
                          d["x_in"][cc * 128:(cc + 1) * 128, s * T:(s + 1) * T])],
                        writes=[f"x_{cc}"])
            cur = s
        {"l0ffn": plan_l0ffn, "l0attn": plan_l0attn,
         "l1ffn": plan_l1ffn, "l1attn": plan_l1attn}[ph](p, ar, pb, d, g, st)
    p.dstep("gpsimd",
            [(d["y_out"][cc * 128:(cc + 1) * 128, cur * T:(cur + 1) * T], st["x"][cc])
             for cc in range(CC)],
            reads=[f"x_{cc}" for cc in range(CC)], writes=["y_out"])
    p.fence()




def plan_ln(p, ar, pb, g, xch, xn, xnname, off, t0=0, ntc=T // 512):
    xbf = ar.alloc_many("lnxbf", 2, 128, 512, BF16)
    xsq = ar.alloc_many("lnxsq", 2, 128, 512, BF16)
    eps = ar.alloc("lneps", 1, 1, F32)
    p.step("vector", lambda v: v.memset(eps, 1e-5), writes=["lneps"])
    mrow = ar.alloc("lnmrow", 1, 512, F32)
    m2 = ar.alloc("lnm2", 1, 512, F32)
    vrow = ar.alloc("lnvrow", 1, 512, F32)
    rstd = ar.alloc("lnrstd", 1, 512, F32)
    xtmp = ar.alloc("lnxtmp", 128, 512, F32)
    psA, psB = pb[0][0:1, 0:512], pb[1][0:1, 0:512]
    psM, psR = pb[2][:, 0:512], pb[3][:, 0:512]

    for tc in range(ntc):
        sl = slice(t0 + tc * 512, t0 + tc * 512 + 512)
        for cc in range(CC):
            pa = cc % 2
            p.step("vector", lambda v, cc=cc, sl=sl, pa=pa: v.tensor_copy(
                out=xbf[pa], in_=xch[cc][:, sl]),
                reads=[f"x_{cc}"], writes=[f"lnxbf{pa}"])
            p.step("scalar", lambda sc, pa=pa: sc.activation(
                out=xsq[pa], in_=xbf[pa], func=AF.Square),
                reads=[f"lnxbf{pa}"], writes=[f"lnxsq{pa}"])
            p.step("tensor", lambda t, cc=cc, pa=pa: t.matmul(
                psA, g["meancol"], xbf[pa],
                start=(cc == 0), stop=(cc == CC - 1)),
                reads=["meancol", f"lnxbf{pa}"], writes=[PN(0)])
            p.step("tensor", lambda t, cc=cc, pa=pa: t.matmul(
                psB, g["meancol"], xsq[pa],
                start=(cc == 0), stop=(cc == CC - 1)),
                reads=["meancol", f"lnxsq{pa}"], writes=[PN(1)])
        p.step("vector", lambda v: v.tensor_copy(out=mrow, in_=psA),
               reads=[PN(0)], writes=["lnmrow"])
        p.step("vector", lambda v: v.tensor_tensor(out=m2, in0=mrow, in1=mrow,
                                                   op=OP.mult),
               reads=["lnmrow"], writes=["lnm2"])
        p.step("vector", lambda v: v.tensor_tensor(out=vrow, in0=psB, in1=m2,
                                                   op=OP.subtract),
               reads=[PN(1), "lnm2"], writes=["lnvrow"])
        p.step("scalar", lambda sc: sc.activation(out=vrow, in_=vrow, func=AF.Ln,
                                                  bias=eps),
               reads=["lnvrow", "lneps"], writes=["lnvrow"])
        p.step("scalar", lambda sc: sc.activation(out=rstd, in_=vrow, func=AF.Exp,
                                                  scale=-0.5),
               reads=["lnvrow"], writes=["lnrstd"])
        p.step("tensor", lambda t: t.matmul(psM, g["negones_row"], mrow,
                                            start=True, stop=True),
               reads=["negones_row", "lnmrow"], writes=[PN(2)])
        p.step("tensor", lambda t: t.matmul(psR, g["ones_row"], rstd,
                                            start=True, stop=True),
               reads=["ones_row", "lnrstd"], writes=[PN(3)])
        for cc in range(CC):
            p.step("vector", lambda v, cc=cc, sl=sl: v.tensor_tensor(
                out=xtmp, in0=xch[cc][:, sl], in1=psM, op=OP.add),
                reads=[f"x_{cc}", PN(2)], writes=["lnxtmp"])
            dsl = slice(off + tc * 512, off + tc * 512 + 512)
            p.step("vector", lambda v, cc=cc, dsl=dsl: v.tensor_tensor(
                out=xn[cc][:, dsl], in0=xtmp, in1=psR, op=OP.mult),
                reads=["lnxtmp", PN(3)], writes=[f"{xnname}{cc}"])
    ar.free_many("lnxbf", 2)
    ar.free_many("lnxsq", 2)
    for nm in ("lnmrow", "lnm2", "lnvrow", "lnrstd", "lnxtmp", "lneps"):
        ar.free(nm)



def plan_l0ffn(p, ar, pb, d, g, st):
    xch = st["x"]
    first = st.get("first", False)
    st["first"] = False
    wall = {nm: ar.alloc(nm, 128, 8192, BF16)
            for nm in ("wrall", "wkall", "wvall")}
    wr = [wall["wrall"][:, cc * 1024:(cc + 1) * 1024] for cc in range(CC)]
    wk = [wall["wkall"][:, cc * 1024:(cc + 1) * 1024] for cc in range(CC)]
    wv = [wall["wvall"][:, cc * 1024:(cc + 1) * 1024] for cc in range(CC)]
    mur = ar.alloc("mur", 128, CC, F32)
    muk = ar.alloc("muk", 128, CC, F32)
    p.dstep("sync", [(mur, d["mu_r"][:, :])], writes=["mur"])
    p.dstep("sync", [(muk, d["mu_k"][:, :])], writes=["muk"])
    if first:
        stg = ar.alloc_many("l0stg", 2, 128, 1024, F32)
        for mi, (mat, nm, sc) in enumerate((("w_r", "wrall", 1.0),
                                            ("w_k", "wkall", 1.0),
                                            ("w_v", "wvall", 0.5))):
            src = d[mat].rearrange("(c p) j -> c p j", p=128)
            dsth = d[mat + "_h"]
            for gi in range(CC):
                i = mi * CC + gi
                pa = i % 2
                p.dstep("scalar", [(stg[pa], src[gi])], writes=[f"l0stg{pa}"],
                        sem=f"da{i % 4}")
                p.step("vector", lambda v, pa=pa, nm=nm, gi=gi, sc=sc:
                       v.tensor_scalar(
                           out=wall[nm][:, gi * 1024:(gi + 1) * 1024],
                           in0=stg[pa], scalar1=sc, scalar2=None, op0=OP.mult),
                       reads=[f"l0stg{pa}"], writes=[nm])
                p.dstep("gpsimd",
                        [(dsth[gi], wall[nm][:, gi * 1024:(gi + 1) * 1024])],
                        reads=[nm], writes=[mat + "_h"], sem=f"dv{i % 2}")
        ar.free_many("l0stg", 2)
    else:
        for mat, nm in (("w_r", "wrall"), ("w_k", "wkall"), ("w_v", "wvall")):
            srch = d[mat + "_h"].rearrange("c p j -> p c j")
            p.dstep("sync",
                    [(wall[nm][:, 0:4096].rearrange("p (c j) -> p c j", j=1024),
                      srch[:, 0:4, :]),
                     (wall[nm][:, 4096:8192].rearrange("p (c j) -> p c j", j=1024),
                      srch[:, 4:8, :])],
                    reads=[mat + "_h"], writes=[nm])

    xn = ar.alloc_many("xn", CC, 128, T + 1, BF16)
    for cc in range(CC):
        p.step("vector", lambda v, cc=cc: v.memset(xn[cc][:, 0:1], 0.0),
               writes=[f"xn{cc}"])
    plan_ln(p, ar, pb, g, xch, xn, "xn", off=1)

    xr = ar.alloc_many("xr", CC, 128, 512, BF16)
    xk = ar.alloc_many("xk", CC, 128, 512, BF16)
    dd1 = ar.alloc("dd", 128, 512, BF16)
    rr = ar.alloc_many("rr", CC, 128, 512, BF16)
    kr = [ar.alloc("krshared", 128, 512, BF16)] * CC
    ka = ar.alloc_many("ka", CC, 128, 512, BF16)
    tmp = ar.alloc("ffntmp", 128, 512, BF16)

    for tc in range(T // 512):
        a = tc * 512
        cur = slice(1 + a, 1 + a + 512)
        pre = slice(a, a + 512)
        for cc in range(CC):
            p.step("vector", lambda v, cc=cc, cur=cur, pre=pre: v.tensor_tensor(
                out=dd1, in0=xn[cc][:, cur], in1=xn[cc][:, pre], op=OP.subtract),
                reads=[f"xn{cc}"], writes=["dd"])
            p.step("vector", lambda v, cc=cc, pre=pre: v.scalar_tensor_tensor(
                out=xr[cc], in0=dd1, scalar=mur[:, cc:cc + 1],
                in1=xn[cc][:, pre], op0=OP.mult, op1=OP.add),
                reads=["dd", "mur", f"xn{cc}"], writes=[f"xr{cc}"])
            p.step("vector", lambda v, cc=cc, pre=pre: v.scalar_tensor_tensor(
                out=xk[cc], in0=dd1, scalar=muk[:, cc:cc + 1],
                in1=xn[cc][:, pre], op0=OP.mult, op1=OP.add),
                reads=["dd", "muk", f"xn{cc}"], writes=[f"xk{cc}"])
        for dt in range(CC):
            bR = 4 + (dt % 2)
            bK = 6 + (dt % 2)
            p.step("tensor",
                   [lambda t, cc=cc, dt=dt, bR=bR: t.matmul(
                       pb[bR][:, 0:512], wr[cc][:, dt * 128:dt * 128 + 128], xr[cc],
                       start=(cc == 0), stop=(cc == CC - 1)) for cc in range(CC)],
                   reads=[f"xr{cc}" for cc in range(CC)] + ["wrall"],
                   writes=[PN(bR)])
            p.step("scalar", lambda sc, dt=dt, bR=bR: sc.activation(
                out=rr[dt], in_=pb[bR][:, 0:512], func=AF.Sigmoid),
                reads=[PN(bR)], writes=[f"rr{dt}"])
            p.step("tensor",
                   [lambda t, cc=cc, dt=dt, bK=bK: t.matmul(
                       pb[bK][:, 0:512], wk[cc][:, dt * 128:dt * 128 + 128], xk[cc],
                       start=(cc == 0), stop=(cc == CC - 1)) for cc in range(CC)],
                   reads=[f"xk{cc}" for cc in range(CC)] + ["wkall"],
                   writes=[PN(bK)])
            p.step("scalar", lambda sc, dt=dt, bK=bK: sc.activation(
                out=kr[dt], in_=pb[bK][:, 0:512], func=AF.Relu),
                reads=[PN(bK)], writes=["krshared"])
            p.step("vector", lambda v, dt=dt: v.tensor_tensor(
                out=ka[dt], in0=kr[dt], in1=kr[dt], op=OP.mult),
                reads=["krshared"], writes=[f"ka{dt}"])
        for dt in range(CC):
            bV = 4 + (dt % 2)
            p.step("tensor",
                   [lambda t, cc=cc, dt=dt, bV=bV: t.matmul(
                       pb[bV][:, 0:512], wv[cc][:, dt * 128:dt * 128 + 128], ka[cc],
                       start=(cc == 0), stop=(cc == CC - 1)) for cc in range(CC)],
                   reads=[f"ka{cc}" for cc in range(CC)] + ["wvall"],
                   writes=[PN(bV)])
            p.step("vector", lambda v, dt=dt, bV=bV: v.tensor_tensor(
                out=tmp, in0=rr[dt], in1=pb[bV][:, 0:512], op=OP.mult),
                reads=[f"rr{dt}", PN(bV)], writes=["ffntmp"])
            radd = "vector" if first else "gpsimd"
            p.step(radd, lambda ge, dt=dt, a=a: ge.tensor_tensor(
                out=xch[dt][:, a:a + 512], in0=xch[dt][:, a:a + 512], in1=tmp,
                op=OP.add),
                reads=["ffntmp", f"x_{dt}"], writes=[f"x_{dt}"])
    for base, cnt in (("xn", CC), ("xr", CC), ("xk", CC), ("rr", CC),
                      ("ka", CC)):
        ar.free_many(base, cnt)
    for nm in ("wrall", "wkall", "wvall", "mur", "muk", "ffntmp", "dd",
               "krshared"):
        ar.free(nm)



def plan_l0attn(p, ar, pb, d, g, st):
    xch = st["x"]
    first = st.get("fA0", False)
    st["fA0"] = False
    wqa = ar.alloc("wqa", 128, 2048, BF16)
    wka = ar.alloc("wka", 128, 2048, BF16)
    wva = ar.alloc("wva", 128, 2048, BF16)
    if first:
        stf = ar.alloc_many("a0stf", 2, 128, 1024, F32)
        ob = ar.alloc_many("a0ob", 2, 128, 1024, BF16)
        i = 0
        for nm, buf, key in (("wq0", wqa, "wqa"), ("wk0", wka, "wka"),
                             ("wv0", wva, "wva")):
            src = d[nm].rearrange("(c p) j -> p c j", p=128)
            for gi in range(2):
                pa = i % 2
                p.dstep("scalar",
                        [(stf[pa].rearrange("p (c j) -> p c j", j=512),
                          src[:, 2 * gi:2 * gi + 2, :])],
                        writes=[f"a0stf{pa}"], sem=f"da{i % 4}")
                p.step("vector", lambda v, pa=pa, buf=buf, gi=gi: v.tensor_copy(
                    out=buf[:, gi * 1024:(gi + 1) * 1024], in_=stf[pa]),
                    reads=[f"a0stf{pa}"], writes=[key])
                p.dstep("gpsimd",
                        [(d[nm + "_h"][:, gi * 1024:(gi + 1) * 1024],
                          buf[:, gi * 1024:(gi + 1) * 1024])],
                    reads=[key], writes=[nm + "_h"], sem=f"dv{i % 2}")
                i += 1
        srco = d["wo0"].rearrange("(c p) j -> c p j", p=128)
        for gi in range(4):
            pa = i % 2
            p.dstep("scalar", [(stf[pa], srco[gi])],
                    writes=[f"a0stf{pa}"], sem=f"da{i % 4}")
            p.step("vector", lambda v, pa=pa: v.tensor_copy(
                out=ob[pa], in_=stf[pa]),
                reads=[f"a0stf{pa}"], writes=[f"a0ob{pa}"])
            p.dstep("gpsimd", [(d["wo0_h"][gi], ob[pa])],
                    reads=[f"a0ob{pa}"], writes=["wo0_h"], sem=f"dv{i % 2}")
            i += 1
    else:
        p.dstep("sync", [(wqa, d["wq0_h"][:, :])], reads=["wq0_h"], writes=["wqa"])
        p.dstep("sync", [(wka, d["wk0_h"][:, :])], reads=["wk0_h"], writes=["wka"])
        p.dstep("sync", [(wva, d["wv0_h"][:, :])], reads=["wv0_h"], writes=["wva"])
    wq = [wqa[:, cc * 512:(cc + 1) * 512] for cc in range(4)]
    wk = [wka[:, cc * 512:(cc + 1) * 512] for cc in range(4)]
    wv = [wva[:, cc * 512:(cc + 1) * 512] for cc in range(4)]
    xn = ar.alloc_many("xn", CC, 128, T, BF16)
    plan_ln(p, ar, pb, g, xch, xn, "xn", off=0)
    if first:
        ar.free_many("a0stf", 2)
        ar.free_many("a0ob", 2)

    qf = ar.alloc_many("qf", 4, 128, T, BF16)
    kf = ar.alloc_many("kf", 4, 128, T, BF16)
    v33 = ar.alloc_many("v33", 16, 128, 528, BF16)
    for tt in range(16):
        p.step("vector", lambda v, tt=tt: v.memset(
            v33[tt].rearrange("p (h w) -> p h w", w=33)[:, :, 32:33], 1.0),
            writes=[f"v33{tt}"])
    for wb, wn, dst, dn in ((wq, "wqa", qf, "qf"), (wk, "wka", kf, "kf")):
        for dt in range(4):
            for tc in range(4):
                bk = (dt + tc) % 4
                sl = slice(tc * 512, tc * 512 + 512)
                p.step("tensor",
                       [lambda t, cc=cc, dt=dt, sl=sl, bk=bk, wb=wb: t.matmul(
                           pb[bk][:, 0:512], wb[cc][:, dt * 128:dt * 128 + 128],
                           xn[cc][:, sl], start=(cc == 0), stop=(cc == 3))
                        for cc in range(4)],
                       reads=[f"xn{cc}" for cc in range(4)] + [wn],
                       writes=[PN(bk)])
                p.step("vector", lambda v, dst=dst, dt=dt, sl=sl, bk=bk:
                       v.tensor_copy(out=dst[dt][:, sl], in_=pb[bk][:, 0:512]),
                       reads=[PN(bk)], writes=[f"{dn}{dt}"])
    for tt in range(16):
        bk = tt % 4
        p.step("tensor",
               [lambda t, cc=cc, tt=tt, bk=bk: t.matmul(
                   pb[bk][:, 0:512], xn[cc][:, tt * 128:tt * 128 + 128], wv[cc],
                   start=(cc == 0), stop=(cc == 3)) for cc in range(4)],
               reads=[f"xn{cc}" for cc in range(4)] + ["wva"],
               writes=[PN(bk)])
        p.step("vector", lambda v, tt=tt, bk=bk: v.tensor_copy(
            out=v33[tt].rearrange("p (h w) -> p h w", w=33)[:, :, 0:32],
            in_=pb[bk][:, 0:512].rearrange("p (h w) -> p h w", w=32)),
            reads=[PN(bk)], writes=[f"v33{tt}"])
    ar.free_many("xn", CC)
    ar.free("wqa")
    ar.free("wka")
    ar.free("wva")

    qsq = ar.alloc("qsq", 128, T, BF16)
    mq = ar.alloc("mq", 4, 8, F32)
    mt = ar.alloc("mt", 4, 1, F32)
    prod = ar.alloc("prod", 32, 4, F32)
    negc = ar.alloc_many("negc", 4, 128, 4, F32)
    for gq in range(4):
        for which, src, sn in ((0, qf, "qf"), (1, kf, "kf")):
            col = gq * 2 + which
            p.step("scalar", lambda sc, src=src, gq=gq: sc.activation(
                out=qsq, in_=src[gq], func=AF.Square),
                reads=[f"{sn}{gq}"], writes=["qsq"])
            for tc in range(4):
                p.step("tensor", lambda t, tc=tc: t.matmul(
                    pb[4][0:4, 0:512], g["headsel"], qsq[:, tc * 512:tc * 512 + 512],
                    start=True, stop=True),
                    reads=["headsel", "qsq"], writes=[PN(4)])
                p.step("vector", lambda v: v.tensor_reduce(
                    out=mt, in_=pb[4][0:4, 0:512], axis=AX.X, op=OP.max),
                    reads=[PN(4)], writes=["mt"])
                if tc == 0:
                    p.step("vector", lambda v, col=col: v.tensor_copy(
                        out=mq[:, col:col + 1], in_=mt),
                        reads=["mt"], writes=["mq"])
                else:
                    p.step("vector", lambda v, col=col: v.tensor_tensor(
                        out=mq[:, col:col + 1], in0=mq[:, col:col + 1], in1=mt,
                        op=OP.max),
                        reads=["mt", "mq"], writes=["mq"])
        p.step("vector", lambda v, gq=gq: v.tensor_tensor(
            out=prod[0:4, gq:gq + 1], in0=mq[:, 2 * gq:2 * gq + 1],
            in1=mq[:, 2 * gq + 1:2 * gq + 2], op=OP.mult),
            reads=["mq"], writes=["prod"])
    p.step("scalar", lambda sc: sc.activation(out=prod[0:4, :], in_=prod[0:4, :],
                                              func=AF.Ln),
           reads=["prod"], writes=["prod"])
    p.step("scalar", lambda sc: sc.activation(out=prod[0:4, :], in_=prod[0:4, :],
                                              func=AF.Exp, scale=0.5),
           reads=["prod"], writes=["prod"])
    p.step("vector", lambda v: v.tensor_scalar(
        out=prod[0:4, :], in0=prod[0:4, :], scalar1=-0.125, scalar2=None,
        op0=OP.mult),
        reads=["prod"], writes=["prod"])
    for gq in range(4):
        for h in range(4):
            for qd in range(4):
                p.step("vector", lambda v, gq=gq, h=h, qd=qd: v.stream_shuffle(
                    out=negc[gq][qd * 32:qd * 32 + 32, h:h + 1],
                    in_=prod[:, gq:gq + 1], mask=[h] * 32),
                    reads=["prod"], writes=[f"negc{gq}"])
    ar.free("qsq")

    for pr in (4, 5):
        p.step("vector", [lambda v, pr=pr: v.memset(pb[pr][32:64, 0:512], 1.0),
                          lambda v, pr=pr: v.memset(pb[pr][96:128, 0:512], 1.0)],
               writes=[PN(pr)])
    et = [ar.alloc_many(f"et{h}_", 16, 128, 512, BF16) for h in range(4)]
    onb = ar.alloc("onb", 128, 512, F32)
    rden = ar.alloc("rden", 128, 512, F32)
    ofm = ar.alloc_many("ofm", 4, 128, T, BF16)

    def av_fns(gq, kt):
        fns = []
        for pair in range(2):
            for j in range(2):
                hloc = 2 * pair + j
                hglob = 4 * gq + hloc
                out = pb[4 + pair][0:33, 0:512] if j == 0 else pb[4 + pair][64:97, 0:512]
                tp = (0, 0) if j == 0 else (0, 64)
                fns.append(lambda t, kt=kt, hloc=hloc, hglob=hglob, out=out,
                           tp=tp: t.matmul(
                               out, v33[kt][:, 33 * hglob:33 * hglob + 33],
                               et[hloc][kt], start=(kt == 0), stop=(kt == 15),
                               tile_position=tp))
        return (fns, [f"v33{kt}", f"et0_{kt}", f"et1_{kt}", f"et2_{kt}",
                      f"et3_{kt}"], [PN(4), PN(5)])

    def epilogue(gq, qc):
        qsl = slice(qc * 512, qc * 512 + 512)
        for pair in range(2):
            p.step("vector", lambda v, pair=pair: v.stream_shuffle(
                out=rden[(2 * pair) * 32:(2 * pair) * 32 + 32, :],
                in_=pb[4 + pair][32:64, 0:512], mask=[0] * 32),
                reads=[PN(4 + pair)], writes=["rden"])
            p.step("vector", lambda v, pair=pair: v.stream_shuffle(
                out=rden[(2 * pair + 1) * 32:(2 * pair + 1) * 32 + 32, :],
                in_=pb[4 + pair][96:128, 0:512], mask=[0] * 32),
                reads=[PN(4 + pair)], writes=["rden"])
            p.step("vector", lambda v, pair=pair: v.stream_shuffle(
                out=onb[(2 * pair) * 32:(2 * pair) * 32 + 32, :],
                in_=pb[4 + pair][0:32, 0:512], mask=list(range(32))),
                reads=[PN(4 + pair)], writes=["onb"])
            p.step("vector", lambda v, pair=pair: v.stream_shuffle(
                out=onb[(2 * pair + 1) * 32:(2 * pair + 1) * 32 + 32, :],
                in_=pb[4 + pair][64:96, 0:512], mask=list(range(32))),
                reads=[PN(4 + pair)], writes=["onb"])
        p.step("vector", lambda v: v.reciprocal(out=rden, in_=rden),
               reads=["rden"], writes=["rden"])
        p.step("vector", lambda v, gq=gq, qc=qc: v.tensor_tensor(
            out=ofm[gq][:, qc * 512:qc * 512 + 512], in0=onb, in1=rden,
            op=OP.mult),
            reads=["onb", "rden"], writes=[f"ofm{gq}"])

    SCB3 = [(0, 1, 2, 3), (6, 7, 0, 1), (2, 3, 6, 7)]
    for gq in range(4):
        for qc in range(4):
            qsl = slice(qc * 512, qc * 512 + 512)
            for kt in range(16):
                bks = SCB3[kt % 3]
                sfns = [lambda t, kt=kt, hh=hh, gq=gq, qsl=qsl,
                        bk=bks[hh]: t.matmul(
                            pb[bk][:, 0:512],
                            kf[gq][32 * hh:32 * hh + 32, kt * 128:kt * 128 + 128],
                            qf[gq][32 * hh:32 * hh + 32, qsl],
                            start=True, stop=True, tile_position=(32 * hh, 0))
                        for hh in range(4)]
                rd = [f"kf{gq}", f"qf{gq}"]
                wr = [PN(b) for b in bks]
                if kt >= 2:
                    af, ard, awr = av_fns(gq, kt - 2)
                    sfns += af
                    rd += ard
                    wr += awr
                p.step("tensor", sfns, reads=rd, writes=wr)
                for pr in range(2):
                    hhs = (0, 1) if pr == 0 else (2, 3)
                    p.step("scalar",
                           [lambda sc, kt=kt, hh=hh, gq=gq, bk=bks[hh]:
                            sc.activation(
                               out=et[hh][kt], in_=pb[bk][:, 0:512], func=AF.Exp,
                               bias=negc[gq][:, hh:hh + 1], scale=0.125)
                            for hh in hhs],
                           reads=[PN(bks[hhs[0]]), PN(bks[hhs[1]]), f"negc{gq}"],
                           writes=[f"et{hh}_{kt}" for hh in hhs])
            for kt in (14, 15):
                af, ard, awr = av_fns(gq, kt)
                p.step("tensor", af, reads=ard, writes=awr)
            epilogue(gq, qc)
    for h in range(4):
        ar.free_many(f"et{h}_", 16)
    ar.free_many("qf", 4)
    ar.free_many("kf", 4)
    ar.free_many("v33", 16)

    woa = ar.alloc("woa", 128, 4096, BF16)
    wo = [woa[:, cc * 1024:(cc + 1) * 1024] for cc in range(4)]
    bo = ar.alloc("bo", 128, CC, F32)
    p.dstep("sync", [(bo, d["bo0"][:, :])], writes=["bo"])
    p.dstep("sync", [(woa.rearrange("p (c j) -> p c j", j=1024),
                      d["wo0_h"].rearrange("c p j -> p c j"))],
            reads=["wo0_h"], writes=["woa"])

    for dot in range(CC):
        for tc in range(4):
            bk = (dot + tc) % 4
            sl = slice(tc * 512, tc * 512 + 512)
            p.step("tensor",
                   [lambda t, gq=gq, dot=dot, sl=sl, bk=bk: t.matmul(
                       pb[bk][:, 0:512], wo[gq][:, dot * 128:dot * 128 + 128],
                       ofm[gq][:, sl], start=(gq == 0), stop=(gq == 3))
                    for gq in range(4)],
                   reads=[f"ofm{gq}" for gq in range(4)] + ["woa"],
                   writes=[PN(bk)])
            p.step("vector", lambda v, dot=dot, sl=sl, bk=bk: v.scalar_tensor_tensor(
                out=xch[dot][:, sl], in0=pb[bk][:, 0:512], scalar=bo[:, dot:dot + 1],
                in1=xch[dot][:, sl], op0=OP.add, op1=OP.add),
                reads=[PN(bk), "bo", f"x_{dot}"], writes=[f"x_{dot}"])
    ar.free_many("ofm", 4)
    ar.free_many("negc", 4)
    for nm in ("woa", "bo", "mq", "mt", "prod", "onb", "rden"):
        ar.free(nm)



def plan_l1ffn(p, ar, pb, d, g, st):
    xch = st["x"]
    first = st.get("fF1", False)
    st["fF1"] = False
    TH = T // 2
    wds = ar.alloc_many("wds", NHT, 128, 1024, BF16)
    if not first:
        for ht in range(NHT):
            p.dstep("gpsimd", [(wds[ht], d["wdbf"][ht, :, :])],
                    reads=["wdbf"], writes=[f"wds{ht}"], sem="dg1")
    SC = DQS * DQS * 0.5

    def gate_up(half, prep):
        for ht in range(NHT):
            pa = ht % 2
            if prep:
                for wi, (nm, dstn, wb, wn) in enumerate(
                        (("wg", "wgbf", wgs, "wgs"), ("wu", "wubf", wus, "wus"))):
                    src = d[nm][:, ht * 128:(ht + 1) * 128].rearrange(
                        "(cc p) j -> p cc j", p=128)
                    p.dstep("sync",
                            [(stig[wi].rearrange("p (c j) -> p c j", j=128), src)],
                            writes=[f"l1stg{wi}"], sem="ds")
                    p.step("vector", lambda v, wi=wi, wb=wb, pa=pa: v.tensor_copy(
                        out=wb[pa], in_=stig[wi]),
                        reads=[f"l1stg{wi}"], writes=[f"{wn}{pa}"])
                    p.dstep("gpsimd", [(d[dstn][ht, :, :], wb[pa])],
                            reads=[f"{wn}{pa}"], writes=[dstn], sem=f"dv{ht % 2}")
            else:
                p.dstep("sync", [(wgs[pa], d["wgbf"][ht, :, :])],
                        reads=["wgbf"], writes=[f"wgs{pa}"])
                p.dstep("sync", [(wus[pa], d["wubf"][ht, :, :])],
                        reads=["wubf"], writes=[f"wus{pa}"])
            for tq in range(TH // 512):
                sl = slice(tq * 512, tq * 512 + 512)
                bG, bU = 4 + pa, 6 + pa
                p.step("tensor",
                       [lambda t, cc=cc, sl=sl, pa=pa, bG=bG: t.matmul(
                           pb[bG][:, 0:512], wgs[pa][:, cc * 128:cc * 128 + 128],
                           xnh[cc][:, sl], start=(cc == 0), stop=(cc == CC - 1))
                        for cc in range(CC)],
                       reads=[f"wgs{pa}"] + [f"xn{cc}" for cc in range(CC)],
                       writes=[PN(bG)])
                p.step("tensor",
                       [lambda t, cc=cc, sl=sl, pa=pa, bU=bU: t.matmul(
                           pb[bU][:, 0:512], wus[pa][:, cc * 128:cc * 128 + 128],
                           xnh[cc][:, sl], start=(cc == 0), stop=(cc == CC - 1))
                        for cc in range(CC)],
                       reads=[f"wus{pa}"] + [f"xn{cc}" for cc in range(CC)],
                       writes=[PN(bU)])
                p.step("scalar", lambda sc, bG=bG: sc.activation(
                    out=sg, in_=pb[bG][:, 0:512], func=AF.Silu, scale=DQS),
                    reads=[PN(bG)], writes=["sg"])
                p.step("vector", lambda v, ht=ht, sl=sl, bU=bU: v.tensor_tensor(
                    out=hbuf[ht][:, sl], in0=pb[bU][:, 0:512], in1=sg, op=OP.mult),
                    reads=[PN(bU), "sg"], writes=[f"h{ht}"])

    def down(half):
        for tq in range(TH // 512):
            sl = slice(tq * 512, tq * 512 + 512)
            gsl = slice(half * TH + tq * 512, half * TH + tq * 512 + 512)
            for dt in range(CC):
                bk = 4 + dt % 4
                p.step("tensor",
                       [lambda t, ht=ht, dt=dt, sl=sl, bk=bk: t.matmul(
                           pb[bk][:, 0:512], wds[ht][:, dt * 128:dt * 128 + 128],
                           hbuf[ht][:, sl], start=(ht == 0), stop=(ht == NHT - 1))
                        for ht in range(NHT)],
                       reads=[f"wds{ht}" for ht in range(NHT)] +
                             [f"h{ht}" for ht in range(NHT)],
                       writes=[PN(bk)])
                p.step("vector", lambda v, dt=dt, gsl=gsl, bk=bk:
                       v.scalar_tensor_tensor(
                           out=xch[dt][:, gsl], in0=pb[bk][:, 0:512], scalar=SC,
                           in1=xch[dt][:, gsl], op0=OP.mult, op1=OP.add),
                       reads=[PN(bk), f"x_{dt}"], writes=[f"x_{dt}"])

    xnh = ar.alloc_many("xn", CC, 128, TH, BF16)
    plan_ln(p, ar, pb, g, xch, xnh, "xn", off=0, t0=0, ntc=TH // 512)
    hbuf = ar.alloc_many("h", NHT, 128, TH, BF16)
    wgs = ar.alloc_many("wgs", 2, 128, 1024, BF16)
    wus = ar.alloc_many("wus", 2, 128, 1024, BF16)
    sg = ar.alloc("sg", 128, 512, BF16)
    if first:
        stig = ar.alloc_many("l1stg", 2, 128, 1024, I32)
    gate_up(0, first)
    ar.free_many("xn", CC)
    if not first:
        xnh = ar.alloc_many("xn", CC, 128, TH, BF16)
        plan_ln(p, ar, pb, g, xch, xnh, "xn", off=0, t0=TH, ntc=TH // 512)
        down(0)
    else:
        for ht in range(NHT):
            wi = ht % 2
            p.dstep("gpsimd", [(stig[wi], d["wd"][ht * 128:(ht + 1) * 128, :])],
                    writes=[f"l1stg{wi}"], sem="dg1")
            p.step("vector", lambda v, wi=wi, ht=ht: v.tensor_copy(
                out=wds[ht], in_=stig[wi]),
                reads=[f"l1stg{wi}"], writes=[f"wds{ht}"])
            p.dstep("gpsimd", [(d["wdbf"][ht, :, :], wds[ht])],
                    reads=[f"wds{ht}"], writes=["wdbf"], sem=f"dv{ht % 2}")
        down(0)
        ar.free_many("l1stg", 2)
        xnh = ar.alloc_many("xn", CC, 128, TH, BF16)
        plan_ln(p, ar, pb, g, xch, xnh, "xn", off=0, t0=TH, ntc=TH // 512)
    gate_up(1, False)
    down(1)
    ar.free_many("xn", CC)
    ar.free_many("h", NHT)
    ar.free_many("wds", NHT)
    ar.free_many("wgs", 2)
    ar.free_many("wus", 2)
    ar.free("sg")



def plan_l1attn(p, ar, pb, d, g, st):
    import os
    _stage = os.environ.get("L1STAGE", "full")
    xch = st["x"]
    first = st.get("fA1", False)
    st["fA1"] = False
    wqa = ar.alloc("wq1a", 128, 8192, BF16)
    wka = ar.alloc("wk1a", 128, 8192, BF16)
    wva = ar.alloc("wv1a", 128, 8192, BF16)
    if first:
        stf = ar.alloc_many("a1stf", 2, 128, 1024, F32)
        i = 0
        for nm, buf, key in (("wq1", wqa, "wq1a"), ("wk1", wka, "wk1a"),
                             ("wv1", wva, "wv1a")):
            src = d[nm].rearrange("(c p) j -> c p j", p=128)
            for gi in range(CC):
                pa = i % 2
                p.dstep("scalar", [(stf[pa], src[gi])],
                        writes=[f"a1stf{pa}"], sem=f"da{i % 4}")
                p.step("vector", lambda v, pa=pa, buf=buf, gi=gi: v.tensor_copy(
                    out=buf[:, gi * 1024:(gi + 1) * 1024], in_=stf[pa]),
                    reads=[f"a1stf{pa}"], writes=[key])
                p.dstep("gpsimd", [(d[nm + "_h"][gi],
                                    buf[:, gi * 1024:(gi + 1) * 1024])],
                        reads=[key], writes=[nm + "_h"], sem=f"dv{i % 2}")
                i += 1
    else:
        for nm, buf, key in (("wq1", wqa, "wq1a"), ("wk1", wka, "wk1a"),
                             ("wv1", wva, "wv1a")):
            srch = d[nm + "_h"].rearrange("c p j -> p c j")
            p.dstep("sync",
                    [(buf[:, 0:4096].rearrange("p (c j) -> p c j", j=1024),
                      srch[:, 0:4, :]),
                     (buf[:, 4096:8192].rearrange("p (c j) -> p c j", j=1024),
                      srch[:, 4:8, :])],
                    reads=[nm + "_h"], writes=[key])
    wq = [wqa[:, cc * 1024:(cc + 1) * 1024] for cc in range(CC)]
    wk = [wka[:, cc * 1024:(cc + 1) * 1024] for cc in range(CC)]
    wv = [wva[:, cc * 1024:(cc + 1) * 1024] for cc in range(CC)]
    xn = ar.alloc_many("xn", CC, 128, T, BF16)
    plan_ln(p, ar, pb, g, xch, xn, "xn", off=0)
    if first:
        ar.free_many("a1stf", 2)
    p.dstep("gpsimd", [(d["xspill"][cc * 128:(cc + 1) * 128, :], xch[cc])
                       for cc in range(CC)],
            reads=[f"x_{cc}" for cc in range(CC)], writes=["xspill"])
    ar.free_many("x_", CC)

    bq = ar.alloc("bq", 128, CC, F32)
    bk_ = ar.alloc("bk", 128, CC, F32)
    bvs = ar.alloc("bvs", 1, C, F32)
    bvr = ar.alloc("bvr", 1, C, BF16)
    p.dstep("sync", [(bq, d["bq1"][:, :])], writes=["bq"])
    p.dstep("sync", [(bk_, d["bk1"][:, :])], writes=["bk"])
    p.dstep("sync", [(bvs, d["bv1"][:, :])], writes=["bvs"])
    p.step("vector", lambda v: v.tensor_copy(out=bvr, in_=bvs),
           reads=["bvs"], writes=["bvr"])

    qf = ar.alloc_many("qf", CC, 128, T, BF16)
    kf = ar.alloc_many("kf", CC, 128, T, BF16)
    vt = ar.alloc_many("vt", 16, 128, 1024, BF16)
    for wb, wn, dst, dn, bias, bn in ((wq, "wq1a", qf, "qf", bq, "bq"),
                                      (wk, "wk1a", kf, "kf", bk_, "bk")):
        for dt in range(CC):
            for tc in range(4):
                bk = (dt + tc) % 4
                sl = slice(tc * 512, tc * 512 + 512)
                p.step("tensor",
                       [lambda t, cc=cc, dt=dt, sl=sl, bk=bk, wb=wb: t.matmul(
                           pb[bk][:, 0:512], wb[cc][:, dt * 128:dt * 128 + 128],
                           xn[cc][:, sl], start=(cc == 0), stop=(cc == CC - 1))
                        for cc in range(CC)],
                       reads=[f"xn{cc}" for cc in range(CC)] + [wn],
                       writes=[PN(bk)])
                p.step("scalar", lambda sc, dst=dst, dt=dt, sl=sl, bk=bk, bias=bias:
                       sc.activation(out=dst[dt][:, sl], in_=pb[bk][:, 0:512],
                                     func=AF.Identity, bias=bias[:, dt:dt + 1]),
                       reads=[PN(bk), bn], writes=[f"{dn}{dt}"])
    for tt in range(16):
        for ec in range(2):
            bk = (2 * tt + ec) % 4
            esl = slice(ec * 512, ec * 512 + 512)
            fns = [lambda t, ec=ec, bk=bk: t.matmul(
                pb[bk][:, 0:512], g["onesbf_row"], bvr[:, ec * 512:ec * 512 + 512],
                start=True, stop=False)]
            fns += [lambda t, cc=cc, tt=tt, esl=esl, bk=bk: t.matmul(
                pb[bk][:, 0:512], xn[cc][:, tt * 128:tt * 128 + 128],
                wv[cc][:, esl], start=False, stop=(cc == CC - 1))
                for cc in range(CC)]
            p.step("tensor", fns,
                   reads=["onesbf_row", "bvr"] + [f"xn{cc}" for cc in range(CC)] +
                         ["wv1a"],
                   writes=[PN(bk)])
            p.step("scalar", lambda sc, tt=tt, esl=esl, bk=bk: sc.activation(
                out=vt[tt][:, esl], in_=pb[bk][:, 0:512], func=AF.Identity),
                reads=[PN(bk)], writes=[f"vt{tt}"])
    ar.free_many("xn", CC)
    ar.free("wq1a")
    ar.free("wk1a")
    ar.free("wv1a")

    if _stage == "qkv":
        ar.free_many("qf", CC)
        ar.free_many("kf", CC)
        ar.free_many("vt", 16)
        for nm in ("bq", "bk", "bvr", "bvs"):
            ar.free(nm)
        st["x"] = ar.alloc_many("x_", CC, 128, T, F32)
        p.dstep("sync", [(st["x"][cc], d["xspill"][cc * 128:(cc + 1) * 128, :])
                         for cc in range(CC)],
                reads=["xspill"], writes=[f"x_{cc}" for cc in range(CC)])
        return
    g = dict(g)
    g["cmask"] = ar.alloc("cmask", 128, 4 * 512, F32)
    p.dstep("sync", [(g["cmask"].rearrange("p (k f) -> p k f", f=512),
                      d["cmask"].rearrange("(k p) f -> p k f", p=128))],
            writes=["cmask"])
    sst = []
    for J in range(16):
        w = T - 512 * (J // 4)
        sst.append(ar.alloc(f"sst{J}", 128, w, BF16))
    mJ = ar.alloc("mJ", 128, 16, F32)
    mt = ar.alloc("mt", 128, 1, F32)
    dJ = ar.alloc("dJ", 128, 16, F32)
    for I in range(4):
        isl = slice(I * 512, I * 512 + 512)
        for J in range(4 * I + 4):
            bk = 4 + (J % 2)
            p.step("tensor",
                   [lambda t, cc=cc, J=J, isl=isl, bk=bk: t.matmul(
                       pb[bk][:, 0:512], kf[cc][:, J * 128:J * 128 + 128],
                       qf[cc][:, isl], start=(cc == 0), stop=(cc == CC - 1))
                    for cc in range(CC)],
                   reads=[f"kf{cc}" for cc in range(CC)] +
                         [f"qf{cc}" for cc in range(CC)],
                   writes=[PN(bk)])
            if J >= 4 * I and not os.environ.get("L1NOMASK"):
                k = J - 4 * I
                p.step("vector", lambda v, k=k, bk=bk: v.tensor_tensor(
                    out=pb[bk][:, 0:512], in0=pb[bk][:, 0:512],
                    in1=g["cmask"][:, k * 512:k * 512 + 512], op=OP.add),
                    reads=[PN(bk), "cmask"], writes=[PN(bk)])
            if os.environ.get("L1NORED"):
                continue
            p.step("vector", lambda v, bk=bk: v.tensor_reduce(
                out=mt, in_=pb[bk][:, 0:512], axis=AX.X, op=OP.max),
                reads=[PN(bk)], writes=["mt"])
            if I == J // 4:
                p.step("vector", lambda v, J=J: v.tensor_copy(
                    out=mJ[:, J:J + 1], in_=mt), reads=["mt"], writes=["mJ"])
            else:
                p.step("vector", lambda v, J=J: v.tensor_tensor(
                    out=mJ[:, J:J + 1], in0=mJ[:, J:J + 1], in1=mt, op=OP.max),
                    reads=["mt", "mJ"], writes=["mJ"])
            off = 512 * (J // 4)
            p.step("vector", lambda v, J=J, I=I, off=off, bk=bk: v.tensor_copy(
                out=sst[J][:, I * 512 - off:I * 512 - off + 512],
                in_=pb[bk][:, 0:512]),
                reads=[PN(bk)], writes=[f"sst{J}"])
    ar.free_many("qf", CC)
    if _stage == "scores":
        for J in range(16):
            ar.free(f"sst{J}")
        ar.free_many("kf", CC)
        ar.free_many("vt", 16)
        for nm in ("mJ", "mt", "dJ", "bq", "bk", "bvr", "bvs"):
            ar.free(nm)
        st["x"] = ar.alloc_many("x_", CC, 128, T, F32)
        p.dstep("sync", [(st["x"][cc], d["xspill"][cc * 128:(cc + 1) * 128, :])
                         for cc in range(CC)],
                reads=["xspill"], writes=[f"x_{cc}" for cc in range(CC)])
        return
    for J in range(16):
        p.step("vector", lambda v, J=J: v.tensor_scalar(
            out=mJ[:, J:J + 1], in0=mJ[:, J:J + 1], scalar1=-1.0 / 32.0,
            scalar2=None, op0=OP.mult),
            reads=["mJ"], writes=["mJ"])
        p.step("scalar", lambda sc, J=J: sc.activation(
            out=sst[J], in_=sst[J], func=AF.Exp, scale=1.0 / 32.0,
            bias=mJ[:, J:J + 1], accum_out=dJ[:, J:J + 1]),
            reads=[f"sst{J}", "mJ"], writes=[f"sst{J}", "dJ"])
        p.step("vector", lambda v, J=J: v.reciprocal(
            out=dJ[:, J:J + 1], in_=dJ[:, J:J + 1]),
            reads=["dJ"], writes=["dJ"])
        p.step("vector", lambda v, J=J: v.tensor_scalar(
            out=vt[J], in0=vt[J], scalar1=dJ[:, J:J + 1], scalar2=None,
            op0=OP.mult),
            reads=["dJ", f"vt{J}"], writes=[f"vt{J}"])
    st["x"] = ar.alloc_many("x_", CC, 128, T, F32)
    xch = st["x"]
    p.dstep("sync", [(xch[cc], d["xspill"][cc * 128:(cc + 1) * 128, :])
                     for cc in range(CC)],
            reads=["xspill"], writes=[f"x_{cc}" for cc in range(CC)])
    if _stage == "exp":
        for J in range(16):
            ar.free(f"sst{J}")
        ar.free_many("kf", CC)
        ar.free_many("vt", 16)
        for nm in ("mJ", "mt", "dJ", "bq", "bk", "bvr", "bvs"):
            ar.free(nm)
        return
    for I in range(4):
        isl = slice(I * 512, I * 512 + 512)
        for et in range(CC):
            bk = et % 4
            p.step("tensor",
                   [lambda t, J=J, et=et, I=I: t.matmul(
                       pb[et % 4][:, 0:512], vt[J][:, et * 128:et * 128 + 128],
                       sst[J][:, I * 512 - 512 * (J // 4):
                              I * 512 - 512 * (J // 4) + 512],
                       start=(J == 0), stop=(J == 4 * I + 3))
                    for J in range(4 * I + 4)],
                   reads=[f"vt{J}" for J in range(4 * I + 4)] +
                         [f"sst{J}" for J in range(4 * I + 4)],
                   writes=[PN(bk)])
            p.step("vector", lambda v, et=et, isl=isl, bk=bk: v.tensor_tensor(
                out=xch[et][:, isl], in0=pb[bk][:, 0:512], in1=xch[et][:, isl],
                op=OP.add),
                reads=[PN(bk), f"x_{et}"], writes=[f"x_{et}"])
    for J in range(16):
        ar.free(f"sst{J}")
    ar.free_many("kf", CC)
    ar.free_many("vt", 16)
    for nm in ("mJ", "mt", "dJ", "bq", "bk", "bvr", "bvs", "cmask"):
        ar.free(nm)



_cache = {}


def _programs():
    if "progs" in _cache:
        return _cache["progs"]
    if FUSED:
        items = [(ph, s) for s in range(NSEQ)
                 for ph in ("l0ffn", "l0attn", "l0ffn", "l1ffn", "l1attn", "l1ffn")]
        progs = [(build_program(items), ("l0ffn", "l0attn", "l1ffn", "l1attn"))]
    else:
        p1 = build_program([("l0ffn", 0), ("l0ffn", 1)])
        p2 = build_program([("l0attn", 0), ("l0attn", 1)])
        p4 = build_program([("l1ffn", 0), ("l1ffn", 1)])
        p5 = build_program([("l1attn", 0), ("l1attn", 1)])
        progs = [(p1, ("l0ffn",)), (p2, ("l0attn",)), (p1, ("l0ffn",)),
                 (p4, ("l1ffn",)), (p5, ("l1attn",)), (p4, ("l1ffn",))]
    _cache["progs"] = progs
    return progs


def _consts():
    cm = np.zeros((512, 512), dtype=np.float32)
    for k in range(4):
        pp = np.arange(128)[:, None] + 128 * k
        ff = np.arange(512)[None, :]
        cm[k * 128:(k + 1) * 128] = np.where(pp > ff, NEG, 0.0).astype(np.float32)
    hs = np.zeros((128, 4), dtype=np.float32)
    for j in range(4):
        hs[j * 32:(j + 1) * 32, j] = 1.0
    return {
        "ones_row": np.ones((1, 128), np.float32),
        "negones_row": np.full((1, 128), -1.0, np.float32),
        "onesbf_row": np.ones((1, 128), np.float32).astype(ml_dtypes.bfloat16),
        "meancol": np.full((128, 1), 1.0 / 1024.0, np.float32).astype(ml_dtypes.bfloat16),
        "headsel": hs.astype(ml_dtypes.bfloat16),
        "cmask": cm,
    }


def _weight_inputs(inp, phases):
    w = {}
    if "l0ffn" in phases:
        w["w_r"] = np.ascontiguousarray(inp["L0_ffn_W_r"].T)
        w["w_k"] = np.ascontiguousarray(inp["L0_ffn_W_k"].T)
        w["w_v"] = np.ascontiguousarray(inp["L0_ffn_W_v"].T)
        w["mu_r"] = np.ascontiguousarray(np.asarray(inp["L0_ffn_mu_r"]).reshape(CC, 128).T)
        w["mu_k"] = np.ascontiguousarray(np.asarray(inp["L0_ffn_mu_k"]).reshape(CC, 128).T)
    if "l0attn" in phases:
        w["wq0"] = np.ascontiguousarray(inp["L0_attn_Wq"].T)
        w["wk0"] = np.ascontiguousarray(inp["L0_attn_Wk"].T)
        w["wv0"] = np.ascontiguousarray(inp["L0_attn_Wv"].T)
        w["wo0"] = np.ascontiguousarray(inp["L0_attn_Wo"].T)
        w["bo0"] = np.ascontiguousarray(np.asarray(inp["L0_attn_bo"]).reshape(CC, 128).T)
    if "l1attn" in phases:
        w["wq1"] = np.ascontiguousarray(inp["L1_attn_Wq"].T)
        w["wk1"] = np.ascontiguousarray(inp["L1_attn_Wk"].T)
        w["wv1"] = np.ascontiguousarray(inp["L1_attn_Wv"].T)
        w["bq1"] = np.ascontiguousarray(np.asarray(inp["L1_attn_bq"]).reshape(CC, 128).T)
        w["bk1"] = np.ascontiguousarray(np.asarray(inp["L1_attn_bk"]).reshape(CC, 128).T)
        w["bv1"] = np.ascontiguousarray(np.asarray(inp["L1_attn_bv"]).reshape(1, C))
    if "l1ffn" in phases:
        w["wg"] = np.ascontiguousarray(inp["L1_ffn_w_gate"].T)
        w["wu"] = np.ascontiguousarray(inp["L1_ffn_w_up"].T)
        w["wd"] = np.ascontiguousarray(inp["L1_ffn_w_down"].T)
    return w


def kernel(**inp):
    progs = _programs()
    consts = _consts()
    x = np.asarray(inp["L0_attn_x"])
    xcur = [np.ascontiguousarray(
        x[2 * c:2 * c + 2].transpose(2, 0, 1).reshape(C, NSEQ * T))
        for c in range(NCORES)]
    for nc, phases in progs:
        w = _weight_inputs(inp, phases)
        in_maps = []
        for c in range(NCORES):
            m = {"x_in": xcur[c]}
            m.update(consts)
            m.update(w)
            in_maps.append(m)
        res = run_bass_kernel_spmd(nc, in_maps, core_ids=list(range(NCORES)))
        xcur = [np.asarray(res.results[c]["y_out"]) for c in range(NCORES)]
    out = np.empty((16, T, C), dtype=np.float32)
    for c in range(NCORES):
        for s in range(NSEQ):
            out[2 * c + s] = xcur[c][:, s * T:(s + 1) * T].T
    return out


# version 13
# speedup vs baseline: 1.0348x; 1.0348x over previous
import numpy as np
import ml_dtypes
import concourse.bass as bass
import concourse.mybir as mybir
from concourse.bass_utils import run_bass_kernel_spmd

F32 = mybir.dt.float32
BF16 = mybir.dt.bfloat16
I32 = mybir.dt.int32
AF = mybir.ActivationFunctionType
OP = mybir.AluOpType
AX = mybir.AxisListType

FUSED = True

NCORES = 8
T = 2048
NSEQ = 2
C = 1024
CC = 8
HID = 3072
NHT = HID // 128
NEG = -1.0e30
DQS = 1.0 / 127.0

COMPUTE = ("tensor", "vector", "scalar", "gpsimd")
SEMOF = {"tensor": "t", "vector": "v", "scalar": "s", "gpsimd": "g"}
DMASEMS = ("ds", "dg", "da0", "da1", "da2", "da3", "dv0", "dv1", "dg1")


class Prog:

    def __init__(self):
        self.ops = {e: [] for e in COMPUTE + ("sync",)}
        self.cnt = {"t": 0, "v": 0, "s": 0, "g": 0}
        for sm in DMASEMS:
            self.cnt[sm] = 0
        self.mark = {}
        self.sem = {}
        self.wtok = {}
        self.rtok = {}
        self.inherit = {}

    def _wait(self, eng, tok):
        if tok is None:
            return
        sm, v = tok
        if sm == "t" and eng == "tensor":
            return
        key = (eng, sm)
        if self.mark.get(key, -1) >= v:
            return
        self.mark[key] = v
        self.ops[eng].append(lambda e, sm=sm, v=v: e.wait_ge(self.sem[sm], v))

    def _pre(self, eng, reads, writes):
        for r in reads:
            self._wait(eng, self.wtok.get(r))
        for w in writes:
            self._wait(eng, self.wtok.get(w))
            for t in self.rtok.get(w, ()):
                self._wait(eng, t)
            for t in self.inherit.pop(w, ()):
                self._wait(eng, t)

    def _post(self, tok, reads, writes):
        for w in writes:
            self.wtok[w] = tok
            self.rtok[w] = []
        for r in reads:
            self.rtok.setdefault(r, []).append(tok)

    def step(self, eng, fns, reads=(), writes=()):
        if not isinstance(fns, (list, tuple)):
            fns = [fns]
        self._pre(eng, reads, writes)
        sm = SEMOF[eng]
        self.cnt[sm] += 1
        v = self.cnt[sm]
        if eng in ("gpsimd", "scalar", "vector", "tensor"):
            for fn in fns:
                self.ops[eng].append(fn)
            self.ops[eng].append(lambda e, sm=sm: e.drain().then_inc(self.sem[sm], 1))
        else:
            for fn in fns[:-1]:
                self.ops[eng].append(fn)
            last = fns[-1]
            self.ops[eng].append(
                lambda e, last=last, sm=sm: last(e).then_inc(self.sem[sm], 1))
        tok = (sm, v)
        self._post(tok, reads, writes)
        return tok

    def dstep(self, queue, transfers, reads=(), writes=(), sem=None):
        sm = sem or ("ds" if queue == "sync" else "dg")
        self._wait(queue, (sm, self.cnt[sm]))
        self._pre(queue, reads, writes)
        for out, in_ in transfers:
            self.cnt[sm] += 16
            self.ops[queue].append(
                lambda e, out=out, in_=in_, sm=sm:
                    e.dma_start(out=out, in_=in_).then_inc(self.sem[sm], 16))
        tok = (sm, self.cnt[sm])
        self._post(tok, reads, writes)
        return tok

    def fence(self):
        toks = []
        for eng in COMPUTE:
            sm = SEMOF[eng]
            self.cnt[sm] += 1
            self.ops[eng].append(lambda e, sm=sm: e.nop().then_inc(self.sem[sm], 1))
            toks.append((sm, self.cnt[sm]))
        for sm in DMASEMS:
            toks.append((sm, self.cnt[sm]))
        for eng in COMPUTE + ("sync",):
            for t in toks:
                if t[0] != SEMOF.get(eng):
                    key = (eng, t[0])
                    if self.mark.get(key, -1) < t[1]:
                        self.mark[key] = t[1]
                        self.ops[eng].append(
                            lambda e, t=t: e.wait_ge(self.sem[t[0]], t[1]))


class Arena:

    def __init__(self, p, ap, nbytes):
        self.p = p
        self.ap = ap
        self.nbytes = nbytes
        self.blocks = {}
        self.ghosts = []
        self.peak = 0

    def alloc(self, name, parts, free, dtype):
        assert name not in self.blocks, name
        sz = (free * mybir.dt.size(dtype) + 31) & ~31
        off = 0
        for boff, bsz in sorted(self.blocks.values()):
            if off + sz <= boff:
                break
            off = max(off, boff + bsz)
        if off + sz > self.nbytes:
            raise MemoryError(f"arena OOM: {name} {sz}B at {off}")
        self.blocks[name] = (off, sz)
        self.peak = max(self.peak, off + sz)
        toks = []
        for goff, gsz, gtoks in self.ghosts:
            if goff < off + sz and off < goff + gsz:
                toks.extend(gtoks)
        if toks:
            self.p.inherit.setdefault(name, []).extend(toks)
        return self.ap[0:parts, off:off + free * mybir.dt.size(dtype)].bitcast(dtype)

    def free(self, name):
        off, sz = self.blocks.pop(name)
        toks = list(self.p.rtok.get(name, ()))
        if self.p.wtok.get(name) is not None:
            toks.append(self.p.wtok[name])
        toks.extend(self.p.inherit.pop(name, ()))
        self.p.wtok.pop(name, None)
        self.p.rtok.pop(name, None)
        if toks:
            self.ghosts.append((off, sz, toks))

    def alloc_many(self, base, n, parts, free, dtype):
        return [self.alloc(f"{base}{i}", parts, free, dtype) for i in range(n)]

    def free_many(self, base, n):
        for i in range(n):
            self.free(f"{base}{i}")



def build_program(items):
    import contextlib
    nc = bass.Bass("TRN2", target_bir_lowering=False)
    phases = {ph for ph, _ in items}

    d = {}
    d["x_in"] = nc.dram_tensor("x_in", [C, NSEQ * T], F32, kind="ExternalInput").ap()
    d["y_out"] = nc.dram_tensor("y_out", [C, NSEQ * T], F32, kind="ExternalOutput").ap()
    d["ones_row"] = nc.dram_tensor("ones_row", [1, 128], F32, kind="ExternalInput").ap()
    d["negones_row"] = nc.dram_tensor("negones_row", [1, 128], F32, kind="ExternalInput").ap()
    d["onesbf_row"] = nc.dram_tensor("onesbf_row", [1, 128], BF16, kind="ExternalInput").ap()
    d["meancol"] = nc.dram_tensor("meancol", [128, 1], BF16, kind="ExternalInput").ap()
    d["headsel"] = nc.dram_tensor("headsel", [128, 4], BF16, kind="ExternalInput").ap()
    d["cmask"] = nc.dram_tensor("cmask", [512, 512], F32, kind="ExternalInput").ap()
    if "l0ffn" in phases:
        for n in ("w_r", "w_k", "w_v"):
            d[n] = nc.dram_tensor(n, [C, C], F32, kind="ExternalInput").ap()
            d[n + "_h"] = nc.dram_tensor(n + "_h", [CC, 128, 1024], BF16,
                                         kind="Internal").ap()
        d["mu_r"] = nc.dram_tensor("mu_r", [128, CC], F32, kind="ExternalInput").ap()
        d["mu_k"] = nc.dram_tensor("mu_k", [128, CC], F32, kind="ExternalInput").ap()
    if "l0attn" in phases:
        d["wq0"] = nc.dram_tensor("wq0", [512, 512], F32, kind="ExternalInput").ap()
        d["wk0"] = nc.dram_tensor("wk0", [512, 512], F32, kind="ExternalInput").ap()
        d["wv0"] = nc.dram_tensor("wv0", [512, 512], F32, kind="ExternalInput").ap()
        d["wo0"] = nc.dram_tensor("wo0", [512, C], F32, kind="ExternalInput").ap()
        d["bo0"] = nc.dram_tensor("bo0", [128, CC], F32, kind="ExternalInput").ap()
        d["wq0_h"] = nc.dram_tensor("wq0_h", [128, 2048], BF16, kind="Internal").ap()
        d["wk0_h"] = nc.dram_tensor("wk0_h", [128, 2048], BF16, kind="Internal").ap()
        d["wv0_h"] = nc.dram_tensor("wv0_h", [128, 2048], BF16, kind="Internal").ap()
        d["wo0_h"] = nc.dram_tensor("wo0_h", [4, 128, 1024], BF16, kind="Internal").ap()
    if "l1attn" in phases:
        d["wq1"] = nc.dram_tensor("wq1", [C, C], F32, kind="ExternalInput").ap()
        d["wk1"] = nc.dram_tensor("wk1", [C, C], F32, kind="ExternalInput").ap()
        d["wv1"] = nc.dram_tensor("wv1", [C, C], F32, kind="ExternalInput").ap()
        d["bq1"] = nc.dram_tensor("bq1", [128, CC], F32, kind="ExternalInput").ap()
        d["bk1"] = nc.dram_tensor("bk1", [128, CC], F32, kind="ExternalInput").ap()
        d["bv1"] = nc.dram_tensor("bv1", [1, C], F32, kind="ExternalInput").ap()
        d["wq1_h"] = nc.dram_tensor("wq1_h", [CC, 128, 1024], BF16, kind="Internal").ap()
        d["wk1_h"] = nc.dram_tensor("wk1_h", [CC, 128, 1024], BF16, kind="Internal").ap()
        d["wv1_h"] = nc.dram_tensor("wv1_h", [CC, 128, 1024], BF16, kind="Internal").ap()
    if "l1ffn" in phases:
        d["wg"] = nc.dram_tensor("wg", [C, HID], I32, kind="ExternalInput").ap()
        d["wu"] = nc.dram_tensor("wu", [C, HID], I32, kind="ExternalInput").ap()
        d["wd"] = nc.dram_tensor("wd", [HID, C], I32, kind="ExternalInput").ap()
        d["wgbf"] = nc.dram_tensor("wgbf", [NHT, 128, 1024], BF16, kind="Internal").ap()
        d["wubf"] = nc.dram_tensor("wubf", [NHT, 128, 1024], BF16, kind="Internal").ap()
        d["wdbf"] = nc.dram_tensor("wdbf", [NHT, 128, 1024], BF16, kind="Internal").ap()
    if "l1ffn" in phases or "l1attn" in phases:
        d["xspill"] = nc.dram_tensor("xspill", [C, T], F32, kind="Internal").ap()

    p = Prog()
    ARENA_B = 206 * 1024
    with contextlib.ExitStack() as stack:
        arena_t = stack.enter_context(
            nc.sbuf_tensor("arena", [128, ARENA_B], mybir.dt.uint8))
        pbt = [stack.enter_context(nc.psum_tensor(f"pb{i}", [128, 512], F32))
               for i in range(8)]
        for sm in ("t", "v", "s", "g") + DMASEMS:
            p.sem[sm] = stack.enter_context(nc.semaphore(f"sem_{sm}"))
        block = stack.enter_context(nc.Block())
        pb = [x.ap() if hasattr(x, "ap") else x for x in pbt]
        ar = Arena(p, arena_t.ap(), ARENA_B)
        plan_all(p, ar, pb, d, items)
        for eng_name, section in (("sync", block.sync), ("tensor", block.tensor),
                                  ("vector", block.vector), ("scalar", block.scalar),
                                  ("gpsimd", block.gpsimd)):
            ops = p.ops[eng_name]
            def body(e, ops=ops):
                for fn in ops:
                    fn(e)
            section(body)
    return nc


def PN(i):
    return f"pb{i}"


def plan_all(p, ar, pb, d, items):
    phases = {ph for ph, _ in items}
    g = {}
    specs = [("ones_row", 1, 128, F32), ("negones_row", 1, 128, F32),
             ("onesbf_row", 1, 128, BF16), ("meancol", 128, 1, BF16)]
    if "l0attn" in phases:
        specs.append(("headsel", 128, 4, BF16))
    for nm, parts, free, dt in specs:
        g[nm] = ar.alloc(nm, parts, free, dt)
    xfers = [(g["ones_row"], d["ones_row"][:, :]),
             (g["negones_row"], d["negones_row"][:, :]),
             (g["onesbf_row"], d["onesbf_row"][:, :]),
             (g["meancol"], d["meancol"][:, :])]
    if "l0attn" in phases:
        xfers.append((g["headsel"], d["headsel"][:, :]))
    p.dstep("sync", xfers, writes=[s[0] for s in specs])

    st = {"x": ar.alloc_many("x_", CC, 128, T, F32),
          "first": True, "fA0": True, "fF1": True, "fA1": True}
    cur = None
    for idx, (ph, s) in enumerate(items):
        if s != cur:
            for cc in range(CC):
                if cur is not None:
                    p.dstep("gpsimd",
                            [(d["y_out"][cc * 128:(cc + 1) * 128,
                                         cur * T:(cur + 1) * T], st["x"][cc])],
                            reads=[f"x_{cc}"], writes=["y_out"])
                p.dstep("sync",
                        [(st["x"][cc],
                          d["x_in"][cc * 128:(cc + 1) * 128, s * T:(s + 1) * T])],
                        writes=[f"x_{cc}"])
            cur = s
        {"l0ffn": plan_l0ffn, "l0attn": plan_l0attn,
         "l1ffn": plan_l1ffn, "l1attn": plan_l1attn}[ph](p, ar, pb, d, g, st)
    p.dstep("gpsimd",
            [(d["y_out"][cc * 128:(cc + 1) * 128, cur * T:(cur + 1) * T], st["x"][cc])
             for cc in range(CC)],
            reads=[f"x_{cc}" for cc in range(CC)], writes=["y_out"])
    p.fence()




def plan_ln(p, ar, pb, g, xch, xn, xnname, off, t0=0, ntc=T // 512):
    xbf = ar.alloc_many("lnxbf", 2, 128, 512, BF16)
    xsq = ar.alloc_many("lnxsq", 2, 128, 512, BF16)
    eps = ar.alloc("lneps", 1, 1, F32)
    p.step("vector", lambda v: v.memset(eps, 1e-5), writes=["lneps"])
    mrow = ar.alloc("lnmrow", 1, 512, F32)
    m2 = ar.alloc("lnm2", 1, 512, F32)
    vrow = ar.alloc("lnvrow", 1, 512, F32)
    rstd = ar.alloc("lnrstd", 1, 512, F32)
    xtmp = ar.alloc("lnxtmp", 128, 512, F32)
    psA, psB = pb[0][0:1, 0:512], pb[1][0:1, 0:512]
    psM, psR = pb[2][:, 0:512], pb[3][:, 0:512]

    for tc in range(ntc):
        sl = slice(t0 + tc * 512, t0 + tc * 512 + 512)
        for cc in range(CC):
            pa = cc % 2
            p.step("vector", lambda v, cc=cc, sl=sl, pa=pa: v.tensor_copy(
                out=xbf[pa], in_=xch[cc][:, sl]),
                reads=[f"x_{cc}"], writes=[f"lnxbf{pa}"])
            p.step("scalar", lambda sc, pa=pa: sc.activation(
                out=xsq[pa], in_=xbf[pa], func=AF.Square),
                reads=[f"lnxbf{pa}"], writes=[f"lnxsq{pa}"])
            p.step("tensor", lambda t, cc=cc, pa=pa: t.matmul(
                psA, g["meancol"], xbf[pa],
                start=(cc == 0), stop=(cc == CC - 1)),
                reads=["meancol", f"lnxbf{pa}"], writes=[PN(0)])
            p.step("tensor", lambda t, cc=cc, pa=pa: t.matmul(
                psB, g["meancol"], xsq[pa],
                start=(cc == 0), stop=(cc == CC - 1)),
                reads=["meancol", f"lnxsq{pa}"], writes=[PN(1)])
        p.step("vector", lambda v: v.tensor_copy(out=mrow, in_=psA),
               reads=[PN(0)], writes=["lnmrow"])
        p.step("vector", lambda v: v.tensor_tensor(out=m2, in0=mrow, in1=mrow,
                                                   op=OP.mult),
               reads=["lnmrow"], writes=["lnm2"])
        p.step("vector", lambda v: v.tensor_tensor(out=vrow, in0=psB, in1=m2,
                                                   op=OP.subtract),
               reads=[PN(1), "lnm2"], writes=["lnvrow"])
        p.step("scalar", lambda sc: sc.activation(out=vrow, in_=vrow, func=AF.Ln,
                                                  bias=eps),
               reads=["lnvrow", "lneps"], writes=["lnvrow"])
        p.step("scalar", lambda sc: sc.activation(out=rstd, in_=vrow, func=AF.Exp,
                                                  scale=-0.5),
               reads=["lnvrow"], writes=["lnrstd"])
        p.step("tensor", lambda t: t.matmul(psM, g["negones_row"], mrow,
                                            start=True, stop=True),
               reads=["negones_row", "lnmrow"], writes=[PN(2)])
        p.step("tensor", lambda t: t.matmul(psR, g["ones_row"], rstd,
                                            start=True, stop=True),
               reads=["ones_row", "lnrstd"], writes=[PN(3)])
        for cc in range(CC):
            p.step("vector", lambda v, cc=cc, sl=sl: v.tensor_tensor(
                out=xtmp, in0=xch[cc][:, sl], in1=psM, op=OP.add),
                reads=[f"x_{cc}", PN(2)], writes=["lnxtmp"])
            dsl = slice(off + tc * 512, off + tc * 512 + 512)
            p.step("vector", lambda v, cc=cc, dsl=dsl: v.tensor_tensor(
                out=xn[cc][:, dsl], in0=xtmp, in1=psR, op=OP.mult),
                reads=["lnxtmp", PN(3)], writes=[f"{xnname}{cc}"])
    ar.free_many("lnxbf", 2)
    ar.free_many("lnxsq", 2)
    for nm in ("lnmrow", "lnm2", "lnvrow", "lnrstd", "lnxtmp", "lneps"):
        ar.free(nm)



def plan_l0ffn(p, ar, pb, d, g, st):
    xch = st["x"]
    first = st.get("first", False)
    st["first"] = False
    wall = {nm: ar.alloc(nm, 128, 8192, BF16)
            for nm in ("wrall", "wkall", "wvall")}
    wr = [wall["wrall"][:, cc * 1024:(cc + 1) * 1024] for cc in range(CC)]
    wk = [wall["wkall"][:, cc * 1024:(cc + 1) * 1024] for cc in range(CC)]
    wv = [wall["wvall"][:, cc * 1024:(cc + 1) * 1024] for cc in range(CC)]
    mur = ar.alloc("mur", 128, CC, F32)
    muk = ar.alloc("muk", 128, CC, F32)
    p.dstep("sync", [(mur, d["mu_r"][:, :])], writes=["mur"])
    p.dstep("sync", [(muk, d["mu_k"][:, :])], writes=["muk"])
    if first:
        stg = ar.alloc_many("l0stg", 2, 128, 1024, F32)
        for mi, (mat, nm, sc) in enumerate((("w_r", "wrall", 1.0),
                                            ("w_k", "wkall", 1.0),
                                            ("w_v", "wvall", 0.5))):
            src = d[mat].rearrange("(c p) j -> c p j", p=128)
            dsth = d[mat + "_h"]
            for gi in range(CC):
                i = mi * CC + gi
                pa = i % 2
                p.dstep("scalar", [(stg[pa], src[gi])], writes=[f"l0stg{pa}"],
                        sem=f"da{i % 4}")
                p.step("vector", lambda v, pa=pa, nm=nm, gi=gi, sc=sc:
                       v.tensor_scalar(
                           out=wall[nm][:, gi * 1024:(gi + 1) * 1024],
                           in0=stg[pa], scalar1=sc, scalar2=None, op0=OP.mult),
                       reads=[f"l0stg{pa}"], writes=[nm])
                p.dstep("gpsimd",
                        [(dsth[gi], wall[nm][:, gi * 1024:(gi + 1) * 1024])],
                        reads=[nm], writes=[mat + "_h"], sem=f"dv{i % 2}")
        ar.free_many("l0stg", 2)
    else:
        for mat, nm in (("w_r", "wrall"), ("w_k", "wkall"), ("w_v", "wvall")):
            srch = d[mat + "_h"].rearrange("c p j -> p c j")
            p.dstep("sync",
                    [(wall[nm][:, 0:4096].rearrange("p (c j) -> p c j", j=1024),
                      srch[:, 0:4, :]),
                     (wall[nm][:, 4096:8192].rearrange("p (c j) -> p c j", j=1024),
                      srch[:, 4:8, :])],
                    reads=[mat + "_h"], writes=[nm])

    xn = ar.alloc_many("xn", CC, 128, T + 1, BF16)
    for cc in range(CC):
        p.step("vector", lambda v, cc=cc: v.memset(xn[cc][:, 0:1], 0.0),
               writes=[f"xn{cc}"])
    plan_ln(p, ar, pb, g, xch, xn, "xn", off=1)

    xr = ar.alloc_many("xr", CC, 128, 512, BF16)
    xk = ar.alloc_many("xk", CC, 128, 512, BF16)
    dd1 = ar.alloc("dd", 128, 512, BF16)
    rr = ar.alloc_many("rr", CC, 128, 512, BF16)
    kr = [ar.alloc("krshared", 128, 512, BF16)] * CC
    ka = ar.alloc_many("ka", CC, 128, 512, BF16)
    tmp = ar.alloc("ffntmp", 128, 512, BF16)

    for tc in range(T // 512):
        a = tc * 512
        cur = slice(1 + a, 1 + a + 512)
        pre = slice(a, a + 512)
        for cc in range(CC):
            p.step("vector", lambda v, cc=cc, cur=cur, pre=pre: v.tensor_tensor(
                out=dd1, in0=xn[cc][:, cur], in1=xn[cc][:, pre], op=OP.subtract),
                reads=[f"xn{cc}"], writes=["dd"])
            p.step("vector", lambda v, cc=cc, pre=pre: v.scalar_tensor_tensor(
                out=xr[cc], in0=dd1, scalar=mur[:, cc:cc + 1],
                in1=xn[cc][:, pre], op0=OP.mult, op1=OP.add),
                reads=["dd", "mur", f"xn{cc}"], writes=[f"xr{cc}"])
            p.step("vector", lambda v, cc=cc, pre=pre: v.scalar_tensor_tensor(
                out=xk[cc], in0=dd1, scalar=muk[:, cc:cc + 1],
                in1=xn[cc][:, pre], op0=OP.mult, op1=OP.add),
                reads=["dd", "muk", f"xn{cc}"], writes=[f"xk{cc}"])
        for dt in range(CC):
            bR = 4 + (dt % 2)
            bK = 6 + (dt % 2)
            p.step("tensor",
                   [lambda t, cc=cc, dt=dt, bR=bR: t.matmul(
                       pb[bR][:, 0:512], wr[cc][:, dt * 128:dt * 128 + 128], xr[cc],
                       start=(cc == 0), stop=(cc == CC - 1)) for cc in range(CC)],
                   reads=[f"xr{cc}" for cc in range(CC)] + ["wrall"],
                   writes=[PN(bR)])
            p.step("scalar", lambda sc, dt=dt, bR=bR: sc.activation(
                out=rr[dt], in_=pb[bR][:, 0:512], func=AF.Sigmoid),
                reads=[PN(bR)], writes=[f"rr{dt}"])
            p.step("tensor",
                   [lambda t, cc=cc, dt=dt, bK=bK: t.matmul(
                       pb[bK][:, 0:512], wk[cc][:, dt * 128:dt * 128 + 128], xk[cc],
                       start=(cc == 0), stop=(cc == CC - 1)) for cc in range(CC)],
                   reads=[f"xk{cc}" for cc in range(CC)] + ["wkall"],
                   writes=[PN(bK)])
            p.step("scalar", lambda sc, dt=dt, bK=bK: sc.activation(
                out=kr[dt], in_=pb[bK][:, 0:512], func=AF.Relu),
                reads=[PN(bK)], writes=["krshared"])
            p.step("vector", lambda v, dt=dt: v.tensor_tensor(
                out=ka[dt], in0=kr[dt], in1=kr[dt], op=OP.mult),
                reads=["krshared"], writes=[f"ka{dt}"])
        for dt in range(CC):
            bV = 4 + (dt % 2)
            p.step("tensor",
                   [lambda t, cc=cc, dt=dt, bV=bV: t.matmul(
                       pb[bV][:, 0:512], wv[cc][:, dt * 128:dt * 128 + 128], ka[cc],
                       start=(cc == 0), stop=(cc == CC - 1)) for cc in range(CC)],
                   reads=[f"ka{cc}" for cc in range(CC)] + ["wvall"],
                   writes=[PN(bV)])
            p.step("vector", lambda v, dt=dt, bV=bV: v.tensor_tensor(
                out=tmp, in0=rr[dt], in1=pb[bV][:, 0:512], op=OP.mult),
                reads=[f"rr{dt}", PN(bV)], writes=["ffntmp"])
            radd = "vector" if first else "gpsimd"
            p.step(radd, lambda ge, dt=dt, a=a: ge.tensor_tensor(
                out=xch[dt][:, a:a + 512], in0=xch[dt][:, a:a + 512], in1=tmp,
                op=OP.add),
                reads=["ffntmp", f"x_{dt}"], writes=[f"x_{dt}"])
    for base, cnt in (("xn", CC), ("xr", CC), ("xk", CC), ("rr", CC),
                      ("ka", CC)):
        ar.free_many(base, cnt)
    for nm in ("wrall", "wkall", "wvall", "mur", "muk", "ffntmp", "dd",
               "krshared"):
        ar.free(nm)



def plan_l0attn(p, ar, pb, d, g, st):
    xch = st["x"]
    first = st.get("fA0", False)
    st["fA0"] = False
    wqa = ar.alloc("wqa", 128, 2048, BF16)
    wka = ar.alloc("wka", 128, 2048, BF16)
    wva = ar.alloc("wva", 128, 2048, BF16)
    if first:
        stf = ar.alloc_many("a0stf", 2, 128, 1024, F32)
        ob = ar.alloc_many("a0ob", 2, 128, 1024, BF16)
        i = 0
        for nm, buf, key in (("wq0", wqa, "wqa"), ("wk0", wka, "wka"),
                             ("wv0", wva, "wva")):
            src = d[nm].rearrange("(c p) j -> p c j", p=128)
            for gi in range(2):
                pa = i % 2
                p.dstep("scalar",
                        [(stf[pa].rearrange("p (c j) -> p c j", j=512),
                          src[:, 2 * gi:2 * gi + 2, :])],
                        writes=[f"a0stf{pa}"], sem=f"da{i % 4}")
                p.step("vector", lambda v, pa=pa, buf=buf, gi=gi: v.tensor_copy(
                    out=buf[:, gi * 1024:(gi + 1) * 1024], in_=stf[pa]),
                    reads=[f"a0stf{pa}"], writes=[key])
                p.dstep("gpsimd",
                        [(d[nm + "_h"][:, gi * 1024:(gi + 1) * 1024],
                          buf[:, gi * 1024:(gi + 1) * 1024])],
                    reads=[key], writes=[nm + "_h"], sem=f"dv{i % 2}")
                i += 1
        srco = d["wo0"].rearrange("(c p) j -> c p j", p=128)
        for gi in range(4):
            pa = i % 2
            p.dstep("scalar", [(stf[pa], srco[gi])],
                    writes=[f"a0stf{pa}"], sem=f"da{i % 4}")
            p.step("vector", lambda v, pa=pa: v.tensor_copy(
                out=ob[pa], in_=stf[pa]),
                reads=[f"a0stf{pa}"], writes=[f"a0ob{pa}"])
            p.dstep("gpsimd", [(d["wo0_h"][gi], ob[pa])],
                    reads=[f"a0ob{pa}"], writes=["wo0_h"], sem=f"dv{i % 2}")
            i += 1
    else:
        p.dstep("sync", [(wqa, d["wq0_h"][:, :])], reads=["wq0_h"], writes=["wqa"])
        p.dstep("sync", [(wka, d["wk0_h"][:, :])], reads=["wk0_h"], writes=["wka"])
        p.dstep("sync", [(wva, d["wv0_h"][:, :])], reads=["wv0_h"], writes=["wva"])
    wq = [wqa[:, cc * 512:(cc + 1) * 512] for cc in range(4)]
    wk = [wka[:, cc * 512:(cc + 1) * 512] for cc in range(4)]
    wv = [wva[:, cc * 512:(cc + 1) * 512] for cc in range(4)]
    xn = ar.alloc_many("xn", CC, 128, T, BF16)
    plan_ln(p, ar, pb, g, xch, xn, "xn", off=0)
    if first:
        ar.free_many("a0stf", 2)
        ar.free_many("a0ob", 2)

    qf = ar.alloc_many("qf", 4, 128, T, BF16)
    kf = ar.alloc_many("kf", 4, 128, T, BF16)
    v33 = ar.alloc_many("v33", 16, 128, 528, BF16)
    for tt in range(16):
        p.step("vector", lambda v, tt=tt: v.memset(
            v33[tt].rearrange("p (h w) -> p h w", w=33)[:, :, 32:33], 1.0),
            writes=[f"v33{tt}"])
    for wb, wn, dst, dn in ((wq, "wqa", qf, "qf"), (wk, "wka", kf, "kf")):
        for dt in range(4):
            for tc in range(4):
                bk = (dt + tc) % 4
                sl = slice(tc * 512, tc * 512 + 512)
                p.step("tensor",
                       [lambda t, cc=cc, dt=dt, sl=sl, bk=bk, wb=wb: t.matmul(
                           pb[bk][:, 0:512], wb[cc][:, dt * 128:dt * 128 + 128],
                           xn[cc][:, sl], start=(cc == 0), stop=(cc == 3))
                        for cc in range(4)],
                       reads=[f"xn{cc}" for cc in range(4)] + [wn],
                       writes=[PN(bk)])
                p.step("vector", lambda v, dst=dst, dt=dt, sl=sl, bk=bk:
                       v.tensor_copy(out=dst[dt][:, sl], in_=pb[bk][:, 0:512]),
                       reads=[PN(bk)], writes=[f"{dn}{dt}"])
    for tt in range(16):
        bk = tt % 4
        p.step("tensor",
               [lambda t, cc=cc, tt=tt, bk=bk: t.matmul(
                   pb[bk][:, 0:512], xn[cc][:, tt * 128:tt * 128 + 128], wv[cc],
                   start=(cc == 0), stop=(cc == 3)) for cc in range(4)],
               reads=[f"xn{cc}" for cc in range(4)] + ["wva"],
               writes=[PN(bk)])
        p.step("vector", lambda v, tt=tt, bk=bk: v.tensor_copy(
            out=v33[tt].rearrange("p (h w) -> p h w", w=33)[:, :, 0:32],
            in_=pb[bk][:, 0:512].rearrange("p (h w) -> p h w", w=32)),
            reads=[PN(bk)], writes=[f"v33{tt}"])
    ar.free_many("xn", CC)
    ar.free("wqa")
    ar.free("wka")
    ar.free("wva")

    qsq = ar.alloc("qsq", 128, T, BF16)
    mq = ar.alloc("mq", 4, 8, F32)
    mt = ar.alloc("mt", 4, 1, F32)
    prod = ar.alloc("prod", 32, 4, F32)
    negc = ar.alloc_many("negc", 4, 128, 4, F32)
    for gq in range(4):
        for which, src, sn in ((0, qf, "qf"), (1, kf, "kf")):
            col = gq * 2 + which
            p.step("scalar", lambda sc, src=src, gq=gq: sc.activation(
                out=qsq, in_=src[gq], func=AF.Square),
                reads=[f"{sn}{gq}"], writes=["qsq"])
            for tc in range(4):
                p.step("tensor", lambda t, tc=tc: t.matmul(
                    pb[4][0:4, 0:512], g["headsel"], qsq[:, tc * 512:tc * 512 + 512],
                    start=True, stop=True),
                    reads=["headsel", "qsq"], writes=[PN(4)])
                p.step("vector", lambda v: v.tensor_reduce(
                    out=mt, in_=pb[4][0:4, 0:512], axis=AX.X, op=OP.max),
                    reads=[PN(4)], writes=["mt"])
                if tc == 0:
                    p.step("vector", lambda v, col=col: v.tensor_copy(
                        out=mq[:, col:col + 1], in_=mt),
                        reads=["mt"], writes=["mq"])
                else:
                    p.step("vector", lambda v, col=col: v.tensor_tensor(
                        out=mq[:, col:col + 1], in0=mq[:, col:col + 1], in1=mt,
                        op=OP.max),
                        reads=["mt", "mq"], writes=["mq"])
        p.step("vector", lambda v, gq=gq: v.tensor_tensor(
            out=prod[0:4, gq:gq + 1], in0=mq[:, 2 * gq:2 * gq + 1],
            in1=mq[:, 2 * gq + 1:2 * gq + 2], op=OP.mult),
            reads=["mq"], writes=["prod"])
    p.step("scalar", lambda sc: sc.activation(out=prod[0:4, :], in_=prod[0:4, :],
                                              func=AF.Ln),
           reads=["prod"], writes=["prod"])
    p.step("scalar", lambda sc: sc.activation(out=prod[0:4, :], in_=prod[0:4, :],
                                              func=AF.Exp, scale=0.5),
           reads=["prod"], writes=["prod"])
    p.step("vector", lambda v: v.tensor_scalar(
        out=prod[0:4, :], in0=prod[0:4, :], scalar1=-0.125, scalar2=None,
        op0=OP.mult),
        reads=["prod"], writes=["prod"])
    for gq in range(4):
        for h in range(4):
            for qd in range(4):
                p.step("vector", lambda v, gq=gq, h=h, qd=qd: v.stream_shuffle(
                    out=negc[gq][qd * 32:qd * 32 + 32, h:h + 1],
                    in_=prod[:, gq:gq + 1], mask=[h] * 32),
                    reads=["prod"], writes=[f"negc{gq}"])
    ar.free("qsq")

    for pr in (4, 5):
        p.step("vector", [lambda v, pr=pr: v.memset(pb[pr][32:64, 0:512], 1.0),
                          lambda v, pr=pr: v.memset(pb[pr][96:128, 0:512], 1.0)],
               writes=[PN(pr)])
    et = [ar.alloc_many(f"et{h}_", 16, 128, 512, BF16) for h in range(4)]
    onb = ar.alloc("onb", 128, 512, F32)
    rden = ar.alloc("rden", 128, 512, F32)
    ofm = ar.alloc_many("ofm", 4, 128, T, BF16)

    def av_fns(gq, kt):
        fns = []
        for pair in range(2):
            for j in range(2):
                hloc = 2 * pair + j
                hglob = 4 * gq + hloc
                out = pb[4 + pair][0:33, 0:512] if j == 0 else pb[4 + pair][64:97, 0:512]
                tp = (0, 0) if j == 0 else (0, 64)
                fns.append(lambda t, kt=kt, hloc=hloc, hglob=hglob, out=out,
                           tp=tp: t.matmul(
                               out, v33[kt][:, 33 * hglob:33 * hglob + 33],
                               et[hloc][kt], start=(kt == 0), stop=(kt == 15),
                               tile_position=tp))
        return (fns, [f"v33{kt}", f"et0_{kt}", f"et1_{kt}", f"et2_{kt}",
                      f"et3_{kt}"], [PN(4), PN(5)])

    def epilogue(gq, qc):
        qsl = slice(qc * 512, qc * 512 + 512)
        for pair in range(2):
            p.step("vector", lambda v, pair=pair: v.stream_shuffle(
                out=rden[(2 * pair) * 32:(2 * pair) * 32 + 32, :],
                in_=pb[4 + pair][32:64, 0:512], mask=[0] * 32),
                reads=[PN(4 + pair)], writes=["rden"])
            p.step("vector", lambda v, pair=pair: v.stream_shuffle(
                out=rden[(2 * pair + 1) * 32:(2 * pair + 1) * 32 + 32, :],
                in_=pb[4 + pair][96:128, 0:512], mask=[0] * 32),
                reads=[PN(4 + pair)], writes=["rden"])
            p.step("vector", lambda v, pair=pair: v.stream_shuffle(
                out=onb[(2 * pair) * 32:(2 * pair) * 32 + 32, :],
                in_=pb[4 + pair][0:32, 0:512], mask=list(range(32))),
                reads=[PN(4 + pair)], writes=["onb"])
            p.step("vector", lambda v, pair=pair: v.stream_shuffle(
                out=onb[(2 * pair + 1) * 32:(2 * pair + 1) * 32 + 32, :],
                in_=pb[4 + pair][64:96, 0:512], mask=list(range(32))),
                reads=[PN(4 + pair)], writes=["onb"])
        p.step("vector", lambda v: v.reciprocal(out=rden, in_=rden),
               reads=["rden"], writes=["rden"])
        p.step("vector", lambda v, gq=gq, qc=qc: v.tensor_tensor(
            out=ofm[gq][:, qc * 512:qc * 512 + 512], in0=onb, in1=rden,
            op=OP.mult),
            reads=["onb", "rden"], writes=[f"ofm{gq}"])

    SCB = [(0, 1), (2, 3), (6, 7)]
    ip = 0
    for gq in range(4):
        for qc in range(4):
            qsl = slice(qc * 512, qc * 512 + 512)
            for kt in range(16):
                for pr in range(2):
                    hhs = (0, 1) if pr == 0 else (2, 3)
                    bks = SCB[ip % 3]
                    ip += 1
                    p.step("tensor",
                           [lambda t, kt=kt, hh=hh, gq=gq, qsl=qsl,
                            bk=bks[hh % 2]: t.matmul(
                               pb[bk][:, 0:512],
                               kf[gq][32 * hh:32 * hh + 32, kt * 128:kt * 128 + 128],
                               qf[gq][32 * hh:32 * hh + 32, qsl],
                               start=True, stop=True, tile_position=(32 * hh, 0))
                            for hh in hhs],
                           reads=[f"kf{gq}", f"qf{gq}"],
                           writes=[PN(bks[0]), PN(bks[1])])
                    p.step("scalar",
                           [lambda sc, kt=kt, hh=hh, gq=gq, bk=bks[hh % 2]:
                            sc.activation(
                               out=et[hh][kt], in_=pb[bk][:, 0:512], func=AF.Exp,
                               bias=negc[gq][:, hh:hh + 1], scale=0.125)
                            for hh in hhs],
                           reads=[PN(bks[0]), PN(bks[1]), f"negc{gq}"],
                           writes=[f"et{hh}_{kt}" for hh in hhs])
                if kt % 2 == 1 and kt >= 3:
                    af, ard, awr = av_fns(gq, kt - 3)
                    af2, ard2, awr2 = av_fns(gq, kt - 2)
                    p.step("tensor", af + af2, reads=ard + ard2, writes=awr)
            for kt in (14, 15):
                af, ard, awr = av_fns(gq, kt)
                p.step("tensor", af, reads=ard, writes=awr)
            epilogue(gq, qc)
    for h in range(4):
        ar.free_many(f"et{h}_", 16)
    ar.free_many("qf", 4)
    ar.free_many("kf", 4)
    ar.free_many("v33", 16)

    woa = ar.alloc("woa", 128, 4096, BF16)
    wo = [woa[:, cc * 1024:(cc + 1) * 1024] for cc in range(4)]
    bo = ar.alloc("bo", 128, CC, F32)
    p.dstep("sync", [(bo, d["bo0"][:, :])], writes=["bo"])
    p.dstep("sync", [(woa.rearrange("p (c j) -> p c j", j=1024),
                      d["wo0_h"].rearrange("c p j -> p c j"))],
            reads=["wo0_h"], writes=["woa"])

    for dot in range(CC):
        for tc in range(4):
            bk = (dot + tc) % 4
            sl = slice(tc * 512, tc * 512 + 512)
            p.step("tensor",
                   [lambda t, gq=gq, dot=dot, sl=sl, bk=bk: t.matmul(
                       pb[bk][:, 0:512], wo[gq][:, dot * 128:dot * 128 + 128],
                       ofm[gq][:, sl], start=(gq == 0), stop=(gq == 3))
                    for gq in range(4)],
                   reads=[f"ofm{gq}" for gq in range(4)] + ["woa"],
                   writes=[PN(bk)])
            p.step("vector", lambda v, dot=dot, sl=sl, bk=bk: v.scalar_tensor_tensor(
                out=xch[dot][:, sl], in0=pb[bk][:, 0:512], scalar=bo[:, dot:dot + 1],
                in1=xch[dot][:, sl], op0=OP.add, op1=OP.add),
                reads=[PN(bk), "bo", f"x_{dot}"], writes=[f"x_{dot}"])
    ar.free_many("ofm", 4)
    ar.free_many("negc", 4)
    for nm in ("woa", "bo", "mq", "mt", "prod", "onb", "rden"):
        ar.free(nm)



def plan_l1ffn(p, ar, pb, d, g, st):
    xch = st["x"]
    first = st.get("fF1", False)
    st["fF1"] = False
    TH = T // 2
    wds = ar.alloc_many("wds", NHT, 128, 1024, BF16)
    if not first:
        for ht in range(NHT):
            p.dstep("gpsimd", [(wds[ht], d["wdbf"][ht, :, :])],
                    reads=["wdbf"], writes=[f"wds{ht}"], sem="dg1")
    SC = DQS * DQS * 0.5

    def gate_up(half, prep):
        for ht in range(NHT):
            pa = ht % 2
            if prep:
                for wi, (nm, dstn, wb, wn) in enumerate(
                        (("wg", "wgbf", wgs, "wgs"), ("wu", "wubf", wus, "wus"))):
                    src = d[nm][:, ht * 128:(ht + 1) * 128].rearrange(
                        "(cc p) j -> p cc j", p=128)
                    p.dstep("sync",
                            [(stig[wi].rearrange("p (c j) -> p c j", j=128), src)],
                            writes=[f"l1stg{wi}"], sem="ds")
                    p.step("vector", lambda v, wi=wi, wb=wb, pa=pa: v.tensor_copy(
                        out=wb[pa], in_=stig[wi]),
                        reads=[f"l1stg{wi}"], writes=[f"{wn}{pa}"])
                    p.dstep("gpsimd", [(d[dstn][ht, :, :], wb[pa])],
                            reads=[f"{wn}{pa}"], writes=[dstn], sem=f"dv{ht % 2}")
            else:
                p.dstep("sync", [(wgs[pa], d["wgbf"][ht, :, :])],
                        reads=["wgbf"], writes=[f"wgs{pa}"])
                p.dstep("sync", [(wus[pa], d["wubf"][ht, :, :])],
                        reads=["wubf"], writes=[f"wus{pa}"])
            for tq in range(TH // 512):
                sl = slice(tq * 512, tq * 512 + 512)
                bG, bU = 4 + pa, 6 + pa
                p.step("tensor",
                       [lambda t, cc=cc, sl=sl, pa=pa, bG=bG: t.matmul(
                           pb[bG][:, 0:512], wgs[pa][:, cc * 128:cc * 128 + 128],
                           xnh[cc][:, sl], start=(cc == 0), stop=(cc == CC - 1))
                        for cc in range(CC)],
                       reads=[f"wgs{pa}"] + [f"xn{cc}" for cc in range(CC)],
                       writes=[PN(bG)])
                p.step("tensor",
                       [lambda t, cc=cc, sl=sl, pa=pa, bU=bU: t.matmul(
                           pb[bU][:, 0:512], wus[pa][:, cc * 128:cc * 128 + 128],
                           xnh[cc][:, sl], start=(cc == 0), stop=(cc == CC - 1))
                        for cc in range(CC)],
                       reads=[f"wus{pa}"] + [f"xn{cc}" for cc in range(CC)],
                       writes=[PN(bU)])
                p.step("scalar", lambda sc, bG=bG: sc.activation(
                    out=sg, in_=pb[bG][:, 0:512], func=AF.Silu, scale=DQS),
                    reads=[PN(bG)], writes=["sg"])
                p.step("vector", lambda v, ht=ht, sl=sl, bU=bU: v.tensor_tensor(
                    out=hbuf[ht][:, sl], in0=pb[bU][:, 0:512], in1=sg, op=OP.mult),
                    reads=[PN(bU), "sg"], writes=[f"h{ht}"])

    def down(half):
        for tq in range(TH // 512):
            sl = slice(tq * 512, tq * 512 + 512)
            gsl = slice(half * TH + tq * 512, half * TH + tq * 512 + 512)
            for dt in range(CC):
                bk = 4 + dt % 4
                p.step("tensor",
                       [lambda t, ht=ht, dt=dt, sl=sl, bk=bk: t.matmul(
                           pb[bk][:, 0:512], wds[ht][:, dt * 128:dt * 128 + 128],
                           hbuf[ht][:, sl], start=(ht == 0), stop=(ht == NHT - 1))
                        for ht in range(NHT)],
                       reads=[f"wds{ht}" for ht in range(NHT)] +
                             [f"h{ht}" for ht in range(NHT)],
                       writes=[PN(bk)])
                p.step("vector", lambda v, dt=dt, gsl=gsl, bk=bk:
                       v.scalar_tensor_tensor(
                           out=xch[dt][:, gsl], in0=pb[bk][:, 0:512], scalar=SC,
                           in1=xch[dt][:, gsl], op0=OP.mult, op1=OP.add),
                       reads=[PN(bk), f"x_{dt}"], writes=[f"x_{dt}"])

    xnh = ar.alloc_many("xn", CC, 128, TH, BF16)
    plan_ln(p, ar, pb, g, xch, xnh, "xn", off=0, t0=0, ntc=TH // 512)
    hbuf = ar.alloc_many("h", NHT, 128, TH, BF16)
    wgs = ar.alloc_many("wgs", 2, 128, 1024, BF16)
    wus = ar.alloc_many("wus", 2, 128, 1024, BF16)
    sg = ar.alloc("sg", 128, 512, BF16)
    if first:
        stig = ar.alloc_many("l1stg", 2, 128, 1024, I32)
    gate_up(0, first)
    ar.free_many("xn", CC)
    if not first:
        xnh = ar.alloc_many("xn", CC, 128, TH, BF16)
        plan_ln(p, ar, pb, g, xch, xnh, "xn", off=0, t0=TH, ntc=TH // 512)
        down(0)
    else:
        for ht in range(NHT):
            wi = ht % 2
            p.dstep("gpsimd", [(stig[wi], d["wd"][ht * 128:(ht + 1) * 128, :])],
                    writes=[f"l1stg{wi}"], sem="dg1")
            p.step("vector", lambda v, wi=wi, ht=ht: v.tensor_copy(
                out=wds[ht], in_=stig[wi]),
                reads=[f"l1stg{wi}"], writes=[f"wds{ht}"])
            p.dstep("gpsimd", [(d["wdbf"][ht, :, :], wds[ht])],
                    reads=[f"wds{ht}"], writes=["wdbf"], sem=f"dv{ht % 2}")
        down(0)
        ar.free_many("l1stg", 2)
        xnh = ar.alloc_many("xn", CC, 128, TH, BF16)
        plan_ln(p, ar, pb, g, xch, xnh, "xn", off=0, t0=TH, ntc=TH // 512)
    gate_up(1, False)
    down(1)
    ar.free_many("xn", CC)
    ar.free_many("h", NHT)
    ar.free_many("wds", NHT)
    ar.free_many("wgs", 2)
    ar.free_many("wus", 2)
    ar.free("sg")



def plan_l1attn(p, ar, pb, d, g, st):
    import os
    _stage = os.environ.get("L1STAGE", "full")
    xch = st["x"]
    first = st.get("fA1", False)
    st["fA1"] = False
    wqa = ar.alloc("wq1a", 128, 8192, BF16)
    wka = ar.alloc("wk1a", 128, 8192, BF16)
    wva = ar.alloc("wv1a", 128, 8192, BF16)
    if first:
        stf = ar.alloc_many("a1stf", 2, 128, 1024, F32)
        i = 0
        for nm, buf, key in (("wq1", wqa, "wq1a"), ("wk1", wka, "wk1a"),
                             ("wv1", wva, "wv1a")):
            src = d[nm].rearrange("(c p) j -> c p j", p=128)
            for gi in range(CC):
                pa = i % 2
                p.dstep("scalar", [(stf[pa], src[gi])],
                        writes=[f"a1stf{pa}"], sem=f"da{i % 4}")
                p.step("vector", lambda v, pa=pa, buf=buf, gi=gi: v.tensor_copy(
                    out=buf[:, gi * 1024:(gi + 1) * 1024], in_=stf[pa]),
                    reads=[f"a1stf{pa}"], writes=[key])
                p.dstep("gpsimd", [(d[nm + "_h"][gi],
                                    buf[:, gi * 1024:(gi + 1) * 1024])],
                        reads=[key], writes=[nm + "_h"], sem=f"dv{i % 2}")
                i += 1
    else:
        for nm, buf, key in (("wq1", wqa, "wq1a"), ("wk1", wka, "wk1a"),
                             ("wv1", wva, "wv1a")):
            srch = d[nm + "_h"].rearrange("c p j -> p c j")
            p.dstep("sync",
                    [(buf[:, 0:4096].rearrange("p (c j) -> p c j", j=1024),
                      srch[:, 0:4, :]),
                     (buf[:, 4096:8192].rearrange("p (c j) -> p c j", j=1024),
                      srch[:, 4:8, :])],
                    reads=[nm + "_h"], writes=[key])
    wq = [wqa[:, cc * 1024:(cc + 1) * 1024] for cc in range(CC)]
    wk = [wka[:, cc * 1024:(cc + 1) * 1024] for cc in range(CC)]
    wv = [wva[:, cc * 1024:(cc + 1) * 1024] for cc in range(CC)]
    xn = ar.alloc_many("xn", CC, 128, T, BF16)
    plan_ln(p, ar, pb, g, xch, xn, "xn", off=0)
    if first:
        ar.free_many("a1stf", 2)
    p.dstep("gpsimd", [(d["xspill"][cc * 128:(cc + 1) * 128, :], xch[cc])
                       for cc in range(CC)],
            reads=[f"x_{cc}" for cc in range(CC)], writes=["xspill"])
    ar.free_many("x_", CC)

    bq = ar.alloc("bq", 128, CC, F32)
    bk_ = ar.alloc("bk", 128, CC, F32)
    bvs = ar.alloc("bvs", 1, C, F32)
    bvr = ar.alloc("bvr", 1, C, BF16)
    p.dstep("sync", [(bq, d["bq1"][:, :])], writes=["bq"])
    p.dstep("sync", [(bk_, d["bk1"][:, :])], writes=["bk"])
    p.dstep("sync", [(bvs, d["bv1"][:, :])], writes=["bvs"])
    p.step("vector", lambda v: v.tensor_copy(out=bvr, in_=bvs),
           reads=["bvs"], writes=["bvr"])

    qf = ar.alloc_many("qf", CC, 128, T, BF16)
    kf = ar.alloc_many("kf", CC, 128, T, BF16)
    vt = ar.alloc_many("vt", 16, 128, 1024, BF16)
    for wb, wn, dst, dn, bias, bn in ((wq, "wq1a", qf, "qf", bq, "bq"),
                                      (wk, "wk1a", kf, "kf", bk_, "bk")):
        for dt in range(CC):
            for tc in range(4):
                bk = (dt + tc) % 4
                sl = slice(tc * 512, tc * 512 + 512)
                p.step("tensor",
                       [lambda t, cc=cc, dt=dt, sl=sl, bk=bk, wb=wb: t.matmul(
                           pb[bk][:, 0:512], wb[cc][:, dt * 128:dt * 128 + 128],
                           xn[cc][:, sl], start=(cc == 0), stop=(cc == CC - 1))
                        for cc in range(CC)],
                       reads=[f"xn{cc}" for cc in range(CC)] + [wn],
                       writes=[PN(bk)])
                p.step("scalar", lambda sc, dst=dst, dt=dt, sl=sl, bk=bk, bias=bias:
                       sc.activation(out=dst[dt][:, sl], in_=pb[bk][:, 0:512],
                                     func=AF.Identity, bias=bias[:, dt:dt + 1]),
                       reads=[PN(bk), bn], writes=[f"{dn}{dt}"])
    for tt in range(16):
        for ec in range(2):
            bk = (2 * tt + ec) % 4
            esl = slice(ec * 512, ec * 512 + 512)
            fns = [lambda t, ec=ec, bk=bk: t.matmul(
                pb[bk][:, 0:512], g["onesbf_row"], bvr[:, ec * 512:ec * 512 + 512],
                start=True, stop=False)]
            fns += [lambda t, cc=cc, tt=tt, esl=esl, bk=bk: t.matmul(
                pb[bk][:, 0:512], xn[cc][:, tt * 128:tt * 128 + 128],
                wv[cc][:, esl], start=False, stop=(cc == CC - 1))
                for cc in range(CC)]
            p.step("tensor", fns,
                   reads=["onesbf_row", "bvr"] + [f"xn{cc}" for cc in range(CC)] +
                         ["wv1a"],
                   writes=[PN(bk)])
            p.step("scalar", lambda sc, tt=tt, esl=esl, bk=bk: sc.activation(
                out=vt[tt][:, esl], in_=pb[bk][:, 0:512], func=AF.Identity),
                reads=[PN(bk)], writes=[f"vt{tt}"])
    ar.free_many("xn", CC)
    ar.free("wq1a")
    ar.free("wk1a")
    ar.free("wv1a")

    if _stage == "qkv":
        ar.free_many("qf", CC)
        ar.free_many("kf", CC)
        ar.free_many("vt", 16)
        for nm in ("bq", "bk", "bvr", "bvs"):
            ar.free(nm)
        st["x"] = ar.alloc_many("x_", CC, 128, T, F32)
        p.dstep("sync", [(st["x"][cc], d["xspill"][cc * 128:(cc + 1) * 128, :])
                         for cc in range(CC)],
                reads=["xspill"], writes=[f"x_{cc}" for cc in range(CC)])
        return
    g = dict(g)
    g["cmask"] = ar.alloc("cmask", 128, 4 * 512, F32)
    p.dstep("sync", [(g["cmask"].rearrange("p (k f) -> p k f", f=512),
                      d["cmask"].rearrange("(k p) f -> p k f", p=128))],
            writes=["cmask"])
    sst = []
    for J in range(16):
        w = T - 512 * (J // 4)
        sst.append(ar.alloc(f"sst{J}", 128, w, BF16))
    mJ = ar.alloc("mJ", 128, 16, F32)
    mt = ar.alloc("mt", 128, 1, F32)
    dJ = ar.alloc("dJ", 128, 16, F32)
    for I in range(4):
        isl = slice(I * 512, I * 512 + 512)
        for J in range(4 * I + 4):
            bk = 4 + (J % 2)
            p.step("tensor",
                   [lambda t, cc=cc, J=J, isl=isl, bk=bk: t.matmul(
                       pb[bk][:, 0:512], kf[cc][:, J * 128:J * 128 + 128],
                       qf[cc][:, isl], start=(cc == 0), stop=(cc == CC - 1))
                    for cc in range(CC)],
                   reads=[f"kf{cc}" for cc in range(CC)] +
                         [f"qf{cc}" for cc in range(CC)],
                   writes=[PN(bk)])
            if J >= 4 * I and not os.environ.get("L1NOMASK"):
                k = J - 4 * I
                p.step("vector", lambda v, k=k, bk=bk: v.tensor_tensor(
                    out=pb[bk][:, 0:512], in0=pb[bk][:, 0:512],
                    in1=g["cmask"][:, k * 512:k * 512 + 512], op=OP.add),
                    reads=[PN(bk), "cmask"], writes=[PN(bk)])
            if os.environ.get("L1NORED"):
                continue
            p.step("vector", lambda v, bk=bk: v.tensor_reduce(
                out=mt, in_=pb[bk][:, 0:512], axis=AX.X, op=OP.max),
                reads=[PN(bk)], writes=["mt"])
            if I == J // 4:
                p.step("vector", lambda v, J=J: v.tensor_copy(
                    out=mJ[:, J:J + 1], in_=mt), reads=["mt"], writes=["mJ"])
            else:
                p.step("vector", lambda v, J=J: v.tensor_tensor(
                    out=mJ[:, J:J + 1], in0=mJ[:, J:J + 1], in1=mt, op=OP.max),
                    reads=["mt", "mJ"], writes=["mJ"])
            off = 512 * (J // 4)
            p.step("vector", lambda v, J=J, I=I, off=off, bk=bk: v.tensor_copy(
                out=sst[J][:, I * 512 - off:I * 512 - off + 512],
                in_=pb[bk][:, 0:512]),
                reads=[PN(bk)], writes=[f"sst{J}"])
    ar.free_many("qf", CC)
    if _stage == "scores":
        for J in range(16):
            ar.free(f"sst{J}")
        ar.free_many("kf", CC)
        ar.free_many("vt", 16)
        for nm in ("mJ", "mt", "dJ", "bq", "bk", "bvr", "bvs"):
            ar.free(nm)
        st["x"] = ar.alloc_many("x_", CC, 128, T, F32)
        p.dstep("sync", [(st["x"][cc], d["xspill"][cc * 128:(cc + 1) * 128, :])
                         for cc in range(CC)],
                reads=["xspill"], writes=[f"x_{cc}" for cc in range(CC)])
        return
    for J in range(16):
        p.step("vector", lambda v, J=J: v.tensor_scalar(
            out=mJ[:, J:J + 1], in0=mJ[:, J:J + 1], scalar1=-1.0 / 32.0,
            scalar2=None, op0=OP.mult),
            reads=["mJ"], writes=["mJ"])
        p.step("scalar", lambda sc, J=J: sc.activation(
            out=sst[J], in_=sst[J], func=AF.Exp, scale=1.0 / 32.0,
            bias=mJ[:, J:J + 1], accum_out=dJ[:, J:J + 1]),
            reads=[f"sst{J}", "mJ"], writes=[f"sst{J}", "dJ"])
        p.step("vector", lambda v, J=J: v.reciprocal(
            out=dJ[:, J:J + 1], in_=dJ[:, J:J + 1]),
            reads=["dJ"], writes=["dJ"])
        p.step("vector", lambda v, J=J: v.tensor_scalar(
            out=vt[J], in0=vt[J], scalar1=dJ[:, J:J + 1], scalar2=None,
            op0=OP.mult),
            reads=["dJ", f"vt{J}"], writes=[f"vt{J}"])
    st["x"] = ar.alloc_many("x_", CC, 128, T, F32)
    xch = st["x"]
    p.dstep("sync", [(xch[cc], d["xspill"][cc * 128:(cc + 1) * 128, :])
                     for cc in range(CC)],
            reads=["xspill"], writes=[f"x_{cc}" for cc in range(CC)])
    if _stage == "exp":
        for J in range(16):
            ar.free(f"sst{J}")
        ar.free_many("kf", CC)
        ar.free_many("vt", 16)
        for nm in ("mJ", "mt", "dJ", "bq", "bk", "bvr", "bvs"):
            ar.free(nm)
        return
    for I in range(4):
        isl = slice(I * 512, I * 512 + 512)
        for et in range(CC):
            bk = et % 4
            p.step("tensor",
                   [lambda t, J=J, et=et, I=I: t.matmul(
                       pb[et % 4][:, 0:512], vt[J][:, et * 128:et * 128 + 128],
                       sst[J][:, I * 512 - 512 * (J // 4):
                              I * 512 - 512 * (J // 4) + 512],
                       start=(J == 0), stop=(J == 4 * I + 3))
                    for J in range(4 * I + 4)],
                   reads=[f"vt{J}" for J in range(4 * I + 4)] +
                         [f"sst{J}" for J in range(4 * I + 4)],
                   writes=[PN(bk)])
            p.step("vector", lambda v, et=et, isl=isl, bk=bk: v.tensor_tensor(
                out=xch[et][:, isl], in0=pb[bk][:, 0:512], in1=xch[et][:, isl],
                op=OP.add),
                reads=[PN(bk), f"x_{et}"], writes=[f"x_{et}"])
    for J in range(16):
        ar.free(f"sst{J}")
    ar.free_many("kf", CC)
    ar.free_many("vt", 16)
    for nm in ("mJ", "mt", "dJ", "bq", "bk", "bvr", "bvs", "cmask"):
        ar.free(nm)



_cache = {}


def _programs():
    if "progs" in _cache:
        return _cache["progs"]
    if FUSED:
        items = [(ph, s) for s in range(NSEQ)
                 for ph in ("l0ffn", "l0attn", "l0ffn", "l1ffn", "l1attn", "l1ffn")]
        progs = [(build_program(items), ("l0ffn", "l0attn", "l1ffn", "l1attn"))]
    else:
        p1 = build_program([("l0ffn", 0), ("l0ffn", 1)])
        p2 = build_program([("l0attn", 0), ("l0attn", 1)])
        p4 = build_program([("l1ffn", 0), ("l1ffn", 1)])
        p5 = build_program([("l1attn", 0), ("l1attn", 1)])
        progs = [(p1, ("l0ffn",)), (p2, ("l0attn",)), (p1, ("l0ffn",)),
                 (p4, ("l1ffn",)), (p5, ("l1attn",)), (p4, ("l1ffn",))]
    _cache["progs"] = progs
    return progs


def _consts():
    cm = np.zeros((512, 512), dtype=np.float32)
    for k in range(4):
        pp = np.arange(128)[:, None] + 128 * k
        ff = np.arange(512)[None, :]
        cm[k * 128:(k + 1) * 128] = np.where(pp > ff, NEG, 0.0).astype(np.float32)
    hs = np.zeros((128, 4), dtype=np.float32)
    for j in range(4):
        hs[j * 32:(j + 1) * 32, j] = 1.0
    return {
        "ones_row": np.ones((1, 128), np.float32),
        "negones_row": np.full((1, 128), -1.0, np.float32),
        "onesbf_row": np.ones((1, 128), np.float32).astype(ml_dtypes.bfloat16),
        "meancol": np.full((128, 1), 1.0 / 1024.0, np.float32).astype(ml_dtypes.bfloat16),
        "headsel": hs.astype(ml_dtypes.bfloat16),
        "cmask": cm,
    }


def _weight_inputs(inp, phases):
    w = {}
    if "l0ffn" in phases:
        w["w_r"] = np.ascontiguousarray(inp["L0_ffn_W_r"].T)
        w["w_k"] = np.ascontiguousarray(inp["L0_ffn_W_k"].T)
        w["w_v"] = np.ascontiguousarray(inp["L0_ffn_W_v"].T)
        w["mu_r"] = np.ascontiguousarray(np.asarray(inp["L0_ffn_mu_r"]).reshape(CC, 128).T)
        w["mu_k"] = np.ascontiguousarray(np.asarray(inp["L0_ffn_mu_k"]).reshape(CC, 128).T)
    if "l0attn" in phases:
        w["wq0"] = np.ascontiguousarray(inp["L0_attn_Wq"].T)
        w["wk0"] = np.ascontiguousarray(inp["L0_attn_Wk"].T)
        w["wv0"] = np.ascontiguousarray(inp["L0_attn_Wv"].T)
        w["wo0"] = np.ascontiguousarray(inp["L0_attn_Wo"].T)
        w["bo0"] = np.ascontiguousarray(np.asarray(inp["L0_attn_bo"]).reshape(CC, 128).T)
    if "l1attn" in phases:
        w["wq1"] = np.ascontiguousarray(inp["L1_attn_Wq"].T)
        w["wk1"] = np.ascontiguousarray(inp["L1_attn_Wk"].T)
        w["wv1"] = np.ascontiguousarray(inp["L1_attn_Wv"].T)
        w["bq1"] = np.ascontiguousarray(np.asarray(inp["L1_attn_bq"]).reshape(CC, 128).T)
        w["bk1"] = np.ascontiguousarray(np.asarray(inp["L1_attn_bk"]).reshape(CC, 128).T)
        w["bv1"] = np.ascontiguousarray(np.asarray(inp["L1_attn_bv"]).reshape(1, C))
    if "l1ffn" in phases:
        w["wg"] = np.ascontiguousarray(inp["L1_ffn_w_gate"].T)
        w["wu"] = np.ascontiguousarray(inp["L1_ffn_w_up"].T)
        w["wd"] = np.ascontiguousarray(inp["L1_ffn_w_down"].T)
    return w


def kernel(**inp):
    progs = _programs()
    consts = _consts()
    x = np.asarray(inp["L0_attn_x"])
    xcur = [np.ascontiguousarray(
        x[2 * c:2 * c + 2].transpose(2, 0, 1).reshape(C, NSEQ * T))
        for c in range(NCORES)]
    for nc, phases in progs:
        w = _weight_inputs(inp, phases)
        in_maps = []
        for c in range(NCORES):
            m = {"x_in": xcur[c]}
            m.update(consts)
            m.update(w)
            in_maps.append(m)
        res = run_bass_kernel_spmd(nc, in_maps, core_ids=list(range(NCORES)))
        xcur = [np.asarray(res.results[c]["y_out"]) for c in range(NCORES)]
    out = np.empty((16, T, C), dtype=np.float32)
    for c in range(NCORES):
        for s in range(NSEQ):
            out[2 * c + s] = xcur[c][:, s * T:(s + 1) * T].T
    return out
